# version 04
# baseline (speedup 1.0000x reference)
import math
import contextlib
import numpy as np
import concourse.bass as bass
import concourse.mybir as mybir
from concourse.bass_utils import run_bass_kernel_spmd


def sb(nc, st, name, shape, dtype=None):
    return st.enter_context(nc.sbuf_tensor(name, shape, dtype or F32))


def ps(nc, st, name, shape, dtype=None):
    return st.enter_context(nc.psum_tensor(name, shape, dtype or F32))

F32 = mybir.dt.float32
BF16 = mybir.dt.bfloat16
U32 = mybir.dt.uint32
AF = mybir.ActivationFunctionType
ALU = mybir.AluOpType
AX = mybir.AxisListType

NCORES = 8
B_ALL, SEQ, D = 16, 2048, 1024
BPC = B_ALL // NCORES
TPC = BPC * SEQ
NTOK = B_ALL * SEQ
H0 = 4
DH0 = D // H0
CHUNK = 128
EPS = 1e-5


class Ser:
    def __init__(self):
        self.q = []
        self.cnt = 0

    def emit(self, ename, fn, inc):
        self.q.append((ename, self.cnt, fn, inc))
        self.cnt += inc

    def dma(self, out, in_, ename="sync", **kw):
        self.emit(ename, lambda e: e.dma_start(out=out, in_=in_, **kw), 16)

    def mm(self, out, lhsT, rhs, start=True, stop=True):
        self.emit("tensor", lambda e: e.matmul(out, lhsT, rhs, start=start, stop=stop), 1)

    def trans(self, out, in_, ident):
        self.emit("tensor", lambda e: e.transpose(out, in_, ident), 1)

    def act(self, out, in_, func, bias=None, scale=None):
        kw = {}
        if bias is not None:
            kw["bias"] = bias
        if scale is not None:
            kw["scale"] = scale
        self.emit("scalar", lambda e: e.activation(out, in_, func, **kw), 1)

    def copy(self, out, in_):
        self.emit("vector", lambda e: e.tensor_copy(out, in_), 1)

    def scopy(self, out, in_):
        self.emit("scalar", lambda e: e.copy(out, in_), 1)

    def tt(self, out, a, b, op):
        self.emit("vector", lambda e: e.tensor_tensor(out, a, b, op), 1)

    def ts(self, out, in_, s1, s2=None, op0=ALU.add, op1=None):
        if s2 is None:
            self.emit("vector", lambda e: e.tensor_scalar(out, in_, s1, None, op0), 1)
        else:
            self.emit("vector", lambda e: e.tensor_scalar(out, in_, s1, s2, op0, op1), 1)

    def stt(self, out, in0, s, in1, op0, op1):
        self.emit("vector", lambda e: e.scalar_tensor_tensor(out, in0, s, in1, op0, op1), 1)

    def red(self, out, in_, op, absval=False, negate=False):
        self.emit("vector", lambda e: e.tensor_reduce(out, in_, AX.X, op,
                                                      apply_absolute_value=absval,
                                                      negate=negate), 1)

    def ttr(self, accum, a, b, scr):
        self.emit("scalar", lambda e: e.activation(scr, a, AF.Square), 1)
        self.emit("vector", lambda e: e.tensor_reduce(scr if False else accum, scr, AX.X, ALU.add), 1)

    def recip(self, out, in_):
        self.emit("vector", lambda e: e.reciprocal(out, in_), 1)

    def memset(self, ap, v):
        self.emit("vector", lambda e: e.memset(ap, v), 1)

    def scan_add(self, out, data0, zeros, initial=0.0):
        self.emit("vector", lambda e: e.tensor_tensor_scan(out, data0, zeros, initial,
                                                           op0=ALU.add, op1=ALU.add), 1)

    def select(self, out, mask, t, f):
        self.emit("vector", lambda e: e.select(out, mask, t, f), 1)

    def idma_gather(self, out, in_, offs):
        self.emit("gpsimd", lambda e: e.indirect_dma_start(
            out, None, in_, offs), 16)

    def idma_scatter(self, out, offs, in_):
        self.emit("gpsimd", lambda e: e.indirect_dma_start(
            out, offs, in_, None), 16)

    def finalize(self, nc):
        with nc.semaphore("gsem") as sem, nc.Block() as blk:
            q = self.q
            total = self.cnt
            for ename in ("sync", "scalar", "vector", "tensor", "gpsimd"):
                ops = [(at, fn, inc) for (en, at, fn, inc) in q if en == ename]
                if not ops and ename != "gpsimd":
                    continue
                dec = getattr(blk, ename)

                def body(eng, ops=ops, ename=ename):
                    for (at, fn, inc) in ops:
                        eng.wait_ge(sem, at)
                        fn(eng).then_inc(sem, inc)
                    if ename == "gpsimd":
                        eng.wait_ge(sem, total)
                dec(body)


def mk_nc():
    return bass.Bass("TRN2", target_bir_lowering=False)


def rne_round_abs(s, out, a, frac_scr):
    s.ts(frac_scr, a, 0.5, 1.0, ALU.add, ALU.mod)
    s.stt(out, a, 0.5, frac_scr, ALU.add, ALU.subtract)



def make_consts():
    c = {}
    c["ident"] = np.eye(128, dtype=np.float32)
    c["ones_col"] = np.ones((128, 1), dtype=np.float32)
    c["ones_row"] = np.ones((1, 128), dtype=np.float32)
    su = np.triu(np.ones((128, 128), dtype=np.float32), 1)
    c["su_mask"] = su
    c["sl_mask"] = su.T.copy()
    c["ui_mask"] = np.triu(np.ones((128, 128), dtype=np.float32), 0)
    cm = np.zeros((128, 128), dtype=np.float32)
    cm[np.triu_indices(128, 1)] = -1e30
    c["causal_add"] = cm
    cc = np.zeros((128, 5), dtype=np.float32)
    cc[:, 0] = 1e-5; cc[:, 1] = 1.0; cc[:, 2] = 0.5; cc[:, 3] = 1e-4; cc[:, 4] = 128.5
    c["ccols"] = cc
    return c


def decl_consts(nc, s, c, need=("ident",)):
    tiles = {}
    handles = {}
    for name in need:
        arr = c[name]
        d = nc.dram_tensor("c_" + name, list(arr.shape), F32, kind="ExternalInput")
        t = nc.sbuf_tensor("cs_" + name, list(arr.shape), F32)
        t.__enter__()
        s.dma(t[:], d.ap())
        tiles[name] = t
        handles[name] = arr
    return tiles, handles



def build_p1():
    nc = mk_nc()
    s = Ser()
    x = nc.dram_tensor("x", [TPC, D], F32, kind="ExternalInput").ap()
    wqT = nc.dram_tensor("wqT", [D, D], F32, kind="ExternalInput").ap()
    wkT = nc.dram_tensor("wkT", [D, D], F32, kind="ExternalInput").ap()
    wvT = nc.dram_tensor("wvT", [D, D], F32, kind="ExternalInput").ap()
    wbT = nc.dram_tensor("wbT", [D, H0], F32, kind="ExternalInput").ap()
    cw = {}
    for nm in ("qcw", "kcw", "vcw"):
        cw[nm] = nc.dram_tensor(nm, [D, 4], F32, kind="ExternalInput").ap()

    xn_o = nc.dram_tensor("xn", [TPC, D], F32, kind="ExternalOutput").ap()
    xnT_o = nc.dram_tensor("xnT", [D, TPC], F32, kind="ExternalOutput").ap()
    qc_o = nc.dram_tensor("qc", [D, TPC], F32, kind="ExternalOutput").ap()
    kc_o = nc.dram_tensor("kc", [D, TPC], F32, kind="ExternalOutput").ap()
    vc_o = nc.dram_tensor("vc", [D, TPC], F32, kind="ExternalOutput").ap()
    beta_o = nc.dram_tensor("beta", [H0, TPC], F32, kind="ExternalOutput").ap()

    with contextlib.ExitStack() as st:
        ident = sb(nc, st, "ident", [128, 128])
        xt = sb(nc, st, "xt", [128, D])
        xnt = sb(nc, st, "xnt", [128, D])
        sq_scr = sb(nc, st, "sq_scr", [128, D])
        st1 = sb(nc, st, "st1", [128, 1])
        st2 = sb(nc, st, "st2", [128, 1])
        st3 = sb(nc, st, "st3", [128, 1])
        wq_s = sb(nc, st, "wq_s", [128, D])
        wb_s = sb(nc, st, "wb_s", [128, H0 * 8])
        xnT_blk = sb(nc, st, "xnT_blk", [128, 8 * 512])
        cwq_t = sb(nc, st, "cwq", [128, 8 * 4])
        cwk_t = sb(nc, st, "cwk", [128, 8 * 4])
        cwv_t = sb(nc, st, "cwv", [128, 8 * 4])
        ext = sb(nc, st, "ext", [128, 3 + 512])
        acc = sb(nc, st, "acc", [128, 512])
        tails = sb(nc, st, "tails", [128, 3 * 24])
        outb = sb(nc, st, "outb", [128, 512])
        psA = ps(nc, st, "psA", [128, 512])
        psB = ps(nc, st, "psB", [128, 512])
        psT = ps(nc, st, "psT", [128, 128])
        cdram = nc.dram_tensor("c_ident", [128, 128], F32, kind="ExternalInput")
        s.dma(ident[:], cdram.ap())
        ccols = sb(nc, st, "ccols", [128, 5])
        s.dma(ccols[:], nc.dram_tensor("c_ccols", [128, 5], F32, kind="ExternalInput").ap())
        epsb = ccols[:, 0:1]
        for j in range(8):
            s.dma(cwq_t[:, j * 4:(j + 1) * 4], cw["qcw"][j * 128:(j + 1) * 128, :])
            s.dma(cwk_t[:, j * 4:(j + 1) * 4], cw["kcw"][j * 128:(j + 1) * 128, :])
            s.dma(cwv_t[:, j * 4:(j + 1) * 4], cw["vcw"][j * 128:(j + 1) * 128, :])

        nblk = TPC // 512
        for blk in range(nblk):
            t0 = blk * 512
            batch_start = (t0 % SEQ) == 0
            for sub in range(4):
                r0 = t0 + sub * 128
                s.dma(xt[:], x[r0:r0 + 128, :])
                s.red(st1[:], xt[:], ALU.add)
                s.ttr(st2[:], xt[:], xt[:], sq_scr[:])
                s.ts(st1[:], st1[:], 1.0 / D, None, ALU.mult)
                s.ts(st2[:], st2[:], 1.0 / D, None, ALU.mult)
                s.emit("scalar", lambda e, o=st3, i=st1: e.square(o[:], i[:]), 1)
                s.tt(st2[:], st2[:], st3[:], ALU.subtract)
                s.act(st3[:], st2[:], AF.Sqrt, bias=epsb)
                s.recip(st3[:], st3[:])
                s.emit("vector", lambda e, o=xnt, i=xt, m=st1, r=st3: e.tensor_scalar(
                    o[:], i[:], m[:], r[:], ALU.subtract, ALU.mult), 1)
                s.dma(xn_o[r0:r0 + 128, :], xnt[:])
                for j in range(8):
                    s.trans(psT[:], xnt[:, j * 128:(j + 1) * 128], ident[:])
                    s.copy(xnT_blk[:, j * 512 + sub * 128: j * 512 + sub * 128 + 128], psT[:])
            for j in range(8):
                s.dma(xnT_o[j * 128:(j + 1) * 128, t0:t0 + 512], xnT_blk[:, j * 512:(j + 1) * 512])

            for (wT, cwt, out_h, psm) in ((wqT, cwq_t, qc_o, psA),
                                          (wkT, cwk_t, kc_o, psB),
                                          (wvT, cwv_t, vc_o, psA)):
                pofs = (0 if out_h is qc_o else (1 if out_h is kc_o else 2)) * 8
                for j in range(8):
                    for kk in range(8):
                        s.dma(wq_s[:, 0:128], wT[kk * 128:(kk + 1) * 128, j * 128:(j + 1) * 128])
                        s.mm(psm[:], wq_s[:, 0:128], xnT_blk[:, kk * 512:(kk + 1) * 512],
                             start=(kk == 0), stop=(kk == 7))
                    tcol = (pofs + j) * 3
                    if batch_start:
                        s.memset(ext[:, 0:3], 0.0)
                    else:
                        s.copy(ext[:, 0:3], tails[:, tcol:tcol + 3])
                    s.copy(ext[:, 3:515], psm[:])
                    s.copy(tails[:, tcol:tcol + 3], ext[:, 512:515])
                    w4 = cwt[:, j * 4 + 3: j * 4 + 4]
                    s.ts(acc[:], ext[:, 3:515], w4, None, ALU.mult)
                    for tap in range(3):
                        wt = cwt[:, j * 4 + tap: j * 4 + tap + 1]
                        s.stt(acc[:], ext[:, tap:tap + 512], wt, acc[:], ALU.mult, ALU.add)
                    s.act(outb[:], acc[:], AF.Silu)
                    s.dma(out_h[j * 128:(j + 1) * 128, t0:t0 + 512], outb[:])

            for kk in range(8):
                s.dma(wb_s[:, kk * H0:(kk + 1) * H0], wbT[kk * 128:(kk + 1) * 128, :])
            for kk in range(8):
                s.mm(psB[0:H0, 0:512], wb_s[:, kk * H0:(kk + 1) * H0],
                     xnT_blk[:, kk * 512:(kk + 1) * 512], start=(kk == 0), stop=(kk == 7))
            s.act(outb[0:H0, 0:512], psB[0:H0, 0:512], AF.Sigmoid)
            s.dma(beta_o[:, t0:t0 + 512], outb[0:H0, 0:512])

        s.finalize(nc)
    return nc


def run_p1(inputs, xfull):
    nc = build_p1()
    consts = make_consts()
    in_maps = []
    for c in range(NCORES):
        m = {
            "x": np.ascontiguousarray(xfull[c * TPC:(c + 1) * TPC]),
            "wqT": np.ascontiguousarray(inputs["L0_attn_q_proj_w"].T),
            "wkT": np.ascontiguousarray(inputs["L0_attn_k_proj_w"].T),
            "wvT": np.ascontiguousarray(inputs["L0_attn_v_proj_w"].T),
            "wbT": np.ascontiguousarray(inputs["L0_attn_b_proj_w"].T),
            "qcw": inputs["L0_attn_q_conv_w"],
            "kcw": inputs["L0_attn_k_conv_w"],
            "vcw": inputs["L0_attn_v_conv_w"],
            "c_ident": consts["ident"],
            "c_ccols": consts["ccols"],
        }
        in_maps.append(m)
    res = run_bass_kernel_spmd(nc, in_maps, core_ids=list(range(NCORES)))
    return res.results



def kernel(**inputs):
    xfull = np.ascontiguousarray(
        np.asarray(inputs["L0_attn_hidden_states"], dtype=np.float32).reshape(NTOK, D))
    r1 = run_p1(inputs, xfull)
    delta = run_p2(r1)
    attn0 = run_p3(inputs, r1, delta)
    r3b = run_p3b(inputs, xfull, r1, attn0)
    attn1 = run_p4(inputs, r3b)
    xn1_full = np.ascontiguousarray(np.concatenate([r["xn1"] for r in r3b], 0))
    probs_full = np.ascontiguousarray(np.concatenate([r["probs"] for r in r3b], 0))
    x1_full = np.concatenate([r["x1"] for r in r3b], 0)
    moe = run_p5_perm(inputs, xn1_full, probs_full)
    out = run_p6(x1_full, attn1, moe)
    return np.concatenate(out, 0).reshape(B_ALL, SEQ, D)


def build_p2(seq_range):
    nc = mk_nc()
    s = Ser()
    qc = nc.dram_tensor("qc", [D, TPC], F32, kind="ExternalInput").ap()
    kc = nc.dram_tensor("kc", [D, TPC], F32, kind="ExternalInput").ap()
    vc = nc.dram_tensor("vc", [D, TPC], F32, kind="ExternalInput").ap()
    beta = nc.dram_tensor("beta", [H0, TPC], F32, kind="ExternalInput").ap()
    delta_o = nc.dram_tensor("delta", [TPC, D], F32, kind="ExternalOutput").ap()

    with contextlib.ExitStack() as st:
        ident = sb(nc, st, "ident", [128, 128])
        ones_row = sb(nc, st, "ones_row", [1, 128])
        ones_col = sb(nc, st, "ones_col", [128, 1])
        su_m = sb(nc, st, "su_m", [128, 128])
        ui_m = sb(nc, st, "ui_m", [128, 128])
        s.dma(ident[:], nc.dram_tensor("c_ident", [128, 128], F32, kind="ExternalInput").ap())
        s.dma(ones_row[:], nc.dram_tensor("c_ones_row", [1, 128], F32, kind="ExternalInput").ap())
        s.dma(ones_col[:], nc.dram_tensor("c_ones_col", [128, 1], F32, kind="ExternalInput").ap())
        s.dma(su_m[:], nc.dram_tensor("c_su_mask", [128, 128], F32, kind="ExternalInput").ap())
        s.dma(ui_m[:], nc.dram_tensor("c_ui_mask", [128, 128], F32, kind="ExternalInput").ap())

        qT2 = sb(nc, st, "qT2", [128, 2 * 2048])
        kT2 = sb(nc, st, "kT2", [128, 2 * 2048])
        vT2 = sb(nc, st, "vT2", [128, 2 * 2048])
        beta_16 = sb(nc, st, "beta_16", [16, 128])
        beta_cm = sb(nc, st, "beta_cm", [128, 16])
        beta_row = sb(nc, st, "beta_row", [1, 2048])
        rq = sb(nc, st, "rq", [1, 2048])
        rk = sb(nc, st, "rk", [1, 2048])
        rkb = sb(nc, st, "rkb", [1, 2048])
        sq_scr = sb(nc, st, "sq_scr", [128, 512])
        S0 = sb(nc, st, "S0", [128, 256])
        S1 = sb(nc, st, "S1", [128, 256])
        qn_s = sb(nc, st, "qn_s", [128, 256])
        kn_s = sb(nc, st, "kn_s", [128, 256])
        kb_s = sb(nc, st, "kb_s", [128, 256])
        kn_tm = sb(nc, st, "kn_tm", [128, 256])
        kb_tm = sb(nc, st, "kb_tm", [128, 256])
        v_tm = sb(nc, st, "v_tm", [128, 256])
        vb_tm = sb(nc, st, "vb_tm", [128, 256])
        U_s = sb(nc, st, "U_s", [128, 128])
        L_s = sb(nc, st, "L_s", [128, 128])
        Wa = sb(nc, st, "Wa", [128, 128])
        Wb = sb(nc, st, "Wb", [128, 128])
        Va = sb(nc, st, "Va", [128, 128])
        Vb = sb(nc, st, "Vb", [128, 128])
        Pa = sb(nc, st, "Pa", [128, 128])
        Pb = sb(nc, st, "Pb", [128, 128])
        u_s = sb(nc, st, "u_s", [128, 256])
        u_adj = sb(nc, st, "u_adj", [128, 256])
        wT_s = sb(nc, st, "wT_s", [128, 256])
        at_s = sb(nc, st, "at_s", [128, 128])
        o_s = sb(nc, st, "o_s", [128, 256])

        bankA = ps(nc, st, "bankA", [128, 512])
        bankB = ps(nc, st, "bankB", [128, 512])
        bankC = ps(nc, st, "bankC", [128, 512])
        bankD = ps(nc, st, "bankD", [128, 512])

        R_bc = bankA[:, 0:128]
        R_mm = bankA[:, 128:256]
        R_tr = bankA[:, 256:384]
        psO = bankB[:, 0:256]
        psO2 = bankB[:, 256:512]
        psO3 = bankC[:, 0:256]
        psS = bankC[:, 256:512]
        ps1 = bankD[0:1, 0:512]

        for (b, h) in seq_range:
            c0 = b * 2048
            r0 = h * 256
            for (dst, src) in ((qT2, qc), (kT2, kc), (vT2, vc)):
                s.dma(dst[:, 0:2048], src[r0:r0 + 128, c0:c0 + 2048])
                s.dma(dst[:, 2048:4096], src[r0 + 128:r0 + 256, c0:c0 + 2048])
            s.dma(beta_16[:], beta[h:h + 1, c0:c0 + 2048].rearrange("o (f p) -> (o f) p", p=128))
            s.trans(R_tr[0:128, 0:16], beta_16[:], ident[0:16, 0:16])
            s.copy(beta_cm[:], R_tr[0:128, 0:16])
            s.dma(beta_row[:], beta[h:h + 1, c0:c0 + 2048])
            for (src, dst) in ((qT2, rq), (kT2, rk)):
                for j4 in range(4):
                    cc = j4 * 512
                    s.act(sq_scr[:], src[:, cc:cc + 512], AF.Square)
                    s.mm(ps1[:, 0:512], ones_col[:], sq_scr[:], start=True, stop=False)
                    s.act(sq_scr[:], src[:, 2048 + cc:2048 + cc + 512], AF.Square)
                    s.mm(ps1[:, 0:512], ones_col[:], sq_scr[:], start=False, stop=True)
                    s.act(dst[:, cc:cc + 512], ps1[:, 0:512], AF.Sqrt)
                    s.recip(dst[:, cc:cc + 512], dst[:, cc:cc + 512])
            s.tt(rkb[:], rk[:], beta_row[:], ALU.mult)
            s.memset(S0[:], 0.0)
            s.memset(S1[:], 0.0)

            for ci in range(16):
                cc = ci * 128
                s.mm(R_bc, ones_row[:], rq[0:1, cc:cc + 128])
                s.tt(qn_s[:, 0:128], qT2[:, cc:cc + 128], R_bc, ALU.mult)
                s.tt(qn_s[:, 128:256], qT2[:, 2048 + cc:2048 + cc + 128], R_bc, ALU.mult)
                s.mm(R_bc, ones_row[:], rk[0:1, cc:cc + 128])
                s.tt(kn_s[:, 0:128], kT2[:, cc:cc + 128], R_bc, ALU.mult)
                s.tt(kn_s[:, 128:256], kT2[:, 2048 + cc:2048 + cc + 128], R_bc, ALU.mult)
                s.mm(R_bc, ones_row[:], rkb[0:1, cc:cc + 128])
                s.tt(kb_s[:, 0:128], kT2[:, cc:cc + 128], R_bc, ALU.mult)
                s.tt(kb_s[:, 128:256], kT2[:, 2048 + cc:2048 + cc + 128], R_bc, ALU.mult)
                for half in range(2):
                    s.trans(R_tr, kn_s[:, half * 128:half * 128 + 128], ident[:])
                    s.copy(kn_tm[:, half * 128:half * 128 + 128], R_tr)
                    s.trans(R_tr, vT2[:, half * 2048 + cc:half * 2048 + cc + 128], ident[:])
                    s.copy(v_tm[:, half * 128:half * 128 + 128], R_tr)
                s.ts(kb_tm[:], kn_tm[:], beta_cm[:, ci:ci + 1], None, ALU.mult)
                s.ts(vb_tm[:], v_tm[:], beta_cm[:, ci:ci + 1], None, ALU.mult)
                s.mm(R_mm, kn_s[:, 0:128], kb_s[:, 0:128], start=True, stop=False)
                s.mm(R_mm, kn_s[:, 128:256], kb_s[:, 128:256], start=False, stop=True)
                s.tt(U_s[:], R_mm, su_m[:], ALU.mult)
                s.trans(R_tr, U_s[:], ident[:])
                s.copy(L_s[:], R_tr)
                s.tt(Pa[:], ident[:], U_s[:], ALU.subtract)
                Wc, Vc = U_s, L_s
                Pcur, Pnxt = Pa, Pb
                Wn, Vn = Wa, Va
                for lev in range(1, 7):
                    s.mm(R_mm, Vc[:], Wc[:], start=True, stop=True)
                    s.copy(Wn[:], R_mm)
                    s.mm(R_mm, Wc[:], Vc[:], start=True, stop=True)
                    s.copy(Vn[:], R_mm)
                    s.mm(R_mm, Vn[:], Pcur[:], start=True, stop=False)
                    s.mm(R_mm, ident[:], Pcur[:], start=False, stop=True)
                    s.copy(Pnxt[:], R_mm)
                    Wc, Vc = Wn, Vn
                    Wn, Vn = (Wb, Vb) if Wn is Wa else (Wa, Va)
                    Pcur, Pnxt = Pnxt, Pcur
                TT = Pcur
                s.mm(psO, TT[:], vb_tm[:], start=True, stop=True)
                s.copy(u_s[:], psO)
                s.mm(R_mm, kb_tm[:, 0:128], TT[:], start=True, stop=True)
                s.copy(wT_s[:, 0:128], R_mm)
                s.mm(R_mm, kb_tm[:, 128:256], TT[:], start=True, stop=True)
                s.copy(wT_s[:, 128:256], R_mm)
                s.mm(psO2, wT_s[:, 0:128], S0[:], start=True, stop=False)
                s.mm(psO2, wT_s[:, 128:256], S1[:], start=False, stop=True)
                s.tt(u_adj[:], u_s[:], psO2, ALU.subtract)
                s.mm(R_mm, kn_s[:, 0:128], qn_s[:, 0:128], start=True, stop=False)
                s.mm(R_mm, kn_s[:, 128:256], qn_s[:, 128:256], start=False, stop=True)
                s.tt(at_s[:], R_mm, ui_m[:], ALU.mult)
                s.mm(psO3, qn_s[:, 0:128], S0[:], start=True, stop=False)
                s.mm(psO3, qn_s[:, 128:256], S1[:], start=False, stop=False)
                s.mm(psO3, at_s[:], u_adj[:], start=False, stop=True)
                s.copy(o_s[:], psO3)
                s.dma(delta_o[b * 2048 + cc: b * 2048 + cc + 128, r0:r0 + 256], o_s[:])
                s.mm(psS, kn_tm[:, 0:128], u_adj[:], start=True, stop=True)
                s.tt(S0[:], S0[:], psS, ALU.add)
                s.mm(psS, kn_tm[:, 128:256], u_adj[:], start=True, stop=True)
                s.tt(S1[:], S1[:], psS, ALU.add)
        s.finalize(nc)
    print("p2 instrs:", len(s.q), "sem total:", s.cnt)
    return nc


def run_p2(r1_results):
    consts = make_consts()
    seqs = [(b, h) for b in range(BPC) for h in range(H0)]
    outs = []
    for part in (seqs[:4], seqs[4:]):
        nc = build_p2(part)
        in_maps = []
        for c in range(NCORES):
            m = {
                "qc": r1_results[c]["qc"],
                "kc": r1_results[c]["kc"],
                "vc": r1_results[c]["vc"],
                "beta": r1_results[c]["beta"],
                "c_ident": consts["ident"],
                "c_ones_row": consts["ones_row"],
                "c_ones_col": consts["ones_col"],
                "c_su_mask": consts["su_mask"],
                "c_ui_mask": consts["ui_mask"],
            }
            in_maps.append(m)
        res = run_bass_kernel_spmd(nc, in_maps, core_ids=list(range(NCORES)))
        outs.append(res.results)
    delta = []
    for c in range(NCORES):
        d = np.zeros((TPC, D), np.float32)
        for (part, rr) in zip((seqs[:4], seqs[4:]), outs):
            for (b, h) in part:
                d[b * 2048:(b + 1) * 2048, h * 256:(h + 1) * 256] = \
                    rr[c]["delta"][b * 2048:(b + 1) * 2048, h * 256:(h + 1) * 256]
        delta.append(d)
    return delta


def build_p3(b):
    nc = mk_nc()
    s = Ser()
    vc = nc.dram_tensor("vc", [D, TPC], F32, kind="ExternalInput").ap()
    delta = nc.dram_tensor("delta", [TPC, D], F32, kind="ExternalInput").ap()
    xnT = nc.dram_tensor("xnT", [D, TPC], F32, kind="ExternalInput").ap()
    firs = nc.dram_tensor("firs", [128, 5 * 8], F32, kind="ExternalInput").ap()
    firl = nc.dram_tensor("firl", [128, 64 * 8], F32, kind="ExternalInput").ap()
    w1aT = nc.dram_tensor("w1aT", [D, D], F32, kind="ExternalInput").ap()
    w1bT = nc.dram_tensor("w1bT", [16, D], F32, kind="ExternalInput").ap()
    b1c = nc.dram_tensor("b1c", [128, 8], F32, kind="ExternalInput").ap()
    w2T = nc.dram_tensor("w2T", [D, 4], F32, kind="ExternalInput").ap()
    b2c = nc.dram_tensor("b2c", [4, 1], F32, kind="ExternalInput").ap()
    ltemp = nc.dram_tensor("ltemp", [1, 1], F32, kind="ExternalInput").ap()
    rss = nc.dram_tensor("rss", [1, 1], F32, kind="ExternalInput").ap()
    rsl = nc.dram_tensor("rsl", [1, 1], F32, kind="ExternalInput").ap()
    opT = nc.dram_tensor("opT", [D, D], F32, kind="ExternalInput").ap()
    onwc = nc.dram_tensor("onwc", [128, 8], F32, kind="ExternalInput").ap()
    attn_o = nc.dram_tensor("attn0", [TPC, D], F32, kind="ExternalOutput").ap()

    with contextlib.ExitStack() as st:
        ident = sb(nc, st, "ident", [128, 128])
        ones_row = sb(nc, st, "ones_row", [1, 128])
        ccols = sb(nc, st, "ccols", [128, 5])
        s.dma(ident[:], nc.dram_tensor("c_ident", [128, 128], F32, kind="ExternalInput").ap())
        s.dma(ones_row[:], nc.dram_tensor("c_ones_row", [1, 128], F32, kind="ExternalInput").ap())
        s.dma(ccols[:], nc.dram_tensor("c_ccols", [128, 5], F32, kind="ExternalInput").ap())
        epsb = ccols[:, 0:1]
        oneb = ccols[:, 1:2]

        vcb = sb(nc, st, "vcb", [128, 8 * 576])
        shf = sb(nc, st, "shf", [128, 8 * 512])
        lof = sb(nc, st, "lof", [128, 8 * 512])
        fir_s = sb(nc, st, "fir_s", [128, 40])
        fir_l = sb(nc, st, "fir_l", [128, 512])
        accf = sb(nc, st, "accf", [128, 512])
        w1a_bf = sb(nc, st, "w1a_bf", [128, 8 * 1024], BF16)
        w2_bf = sb(nc, st, "w2_bf", [128, 4 * 8], BF16)
        w1b_s = sb(nc, st, "w1b_s", [16, D])
        b1_s = sb(nc, st, "b1_s", [128, 8])
        b2_s = sb(nc, st, "b2_s", [4, 1])
        stg = sb(nc, st, "stg", [128, 1024])
        opT_s = sb(nc, st, "opT_s", [128, 8 * 1024])
        onw_s = sb(nc, st, "onw_s", [128, 8])
        xnTf = sb(nc, st, "xnTf", [128, 8 * 512])
        xnTb = sb(nc, st, "xnTb", [128, 8 * 512], BF16)
        H1s = sb(nc, st, "H1s", [128, 8 * 512])
        hdn_bf = sb(nc, st, "hdn_bf", [128, 512], BF16)
        statsT = sb(nc, st, "statsT", [16, 4 * 512])
        stats_all = sb(nc, st, "stats_all", [128, 64])
        sq_scr = sb(nc, st, "sq_scr", [128, 1024])
        sc1 = sb(nc, st, "sc1", [128, 4])
        sc2 = sb(nc, st, "sc2", [128, 4])
        sc3 = sb(nc, st, "sc3", [128, 4])
        vd_tm = sb(nc, st, "vd_tm", [128, 1024])
        sh_tm = sb(nc, st, "sh_tm", [128, 1024])
        lo_tm = sb(nc, st, "lo_tm", [128, 1024])
        dl_tm = sb(nc, st, "dl_tm", [128, 1024])
        lg_s = sb(nc, st, "lg_s", [4, 512])
        fw_s = sb(nc, st, "fw_s", [128, 4 * 4])
        o_tm = sb(nc, st, "o_tm", [128, 1024])
        oT_s = sb(nc, st, "oT_s", [128, 8 * 512])
        sp1 = sb(nc, st, "sp1", [1, 1])
        sp2 = sb(nc, st, "sp2", [1, 1])
        sp3 = sb(nc, st, "sp3", [1, 1])
        rtemp_b = sb(nc, st, "rtemp_b", [128, 1])
        rss_b = sb(nc, st, "rss_b", [128, 1])
        rsl_b = sb(nc, st, "rsl_b", [128, 1])

        bank1 = ps(nc, st, "bank1", [128, 512])
        bank2 = ps(nc, st, "bank2", [128, 512])
        bank3 = ps(nc, st, "bank3", [128, 512], BF16)
        bank4 = ps(nc, st, "bank4", [128, 512])
        R_tr = bank1[:, 0:128]
        R_l4 = bank4[0:4, 0:512]

        s.dma(fir_s[:], firs)
        for jj in range(4):
            s.dma(fir_l[:, jj * 128:(jj + 1) * 128], firl[:, jj * 128:(jj + 1) * 128])
        s.dma(w1b_s[:], w1bT)
        s.dma(b1_s[:], b1c)
        s.dma(b2_s[:], b2c)
        for j in range(8):
            s.dma(stg[:], w1aT[j * 128:(j + 1) * 128, :])
            s.scopy(w1a_bf[:, j * 1024:(j + 1) * 1024], stg[:])
            s.dma(stg[:, 0:4], w2T[j * 128:(j + 1) * 128, :])
            s.scopy(w2_bf[:, j * 4:(j + 1) * 4], stg[:, 0:4])
            s.dma(opT_s[:, j * 1024:(j + 1) * 1024], opT[j * 128:(j + 1) * 128, :])
        s.dma(onw_s[:], onwc)
        for j in range(8):
            s.ts(opT_s[:, j * 1024:(j + 1) * 1024], opT_s[:, j * 1024:(j + 1) * 1024],
                 onw_s[:, j:j + 1], None, ALU.mult)
        s.dma(sp1[:], ltemp)
        s.act(sp2[:], sp1[:], AF.Abs)
        s.act(sp2[:], sp2[:], AF.Exp, scale=-1.0)
        s.act(sp2[:], sp2[:], AF.Ln, bias=oneb[0:1, 0:1])
        s.ts(sp3[:], sp1[:], 0.0, None, ALU.max)
        s.tt(sp3[:], sp3[:], sp2[:], ALU.add)
        s.ts(sp3[:], sp3[:], 1e-4, None, ALU.add)
        s.recip(sp3[:], sp3[:])
        s.mm(bank1[:, 0:1], ones_row[:], sp3[:], start=True, stop=True)
        s.copy(rtemp_b[:], bank1[:, 0:1])
        s.dma(sp1[:], rss)
        s.mm(bank1[:, 0:1], ones_row[:], sp1[:], start=True, stop=True)
        s.copy(rss_b[:], bank1[:, 0:1])
        s.dma(sp1[:], rsl)
        s.mm(bank1[:, 0:1], ones_row[:], sp1[:], start=True, stop=True)
        s.copy(rsl_b[:], bank1[:, 0:1])

        for blk in range(4):
            bc0 = blk * 512
            c0 = b * 2048 + bc0
            t0 = c0
            for j in range(8):
                base = j * 576
                if bc0 == 0:
                    s.memset(vcb[:, base:base + 63], 0.0)
                    s.dma(vcb[:, base + 63:base + 575], vc[j * 128:(j + 1) * 128, c0:c0 + 512])
                else:
                    s.dma(vcb[:, base:base + 575], vc[j * 128:(j + 1) * 128, c0 - 63:c0 + 512])
            for j in range(8):
                base = j * 576
                dst_s = shf[:, j * 512:(j + 1) * 512]
                dst_l = lof[:, j * 512:(j + 1) * 512]
                s.ts(accf[:], vcb[:, base + 59:base + 571], fir_s[:, j * 5 + 0: j * 5 + 1], None, ALU.mult)
                for jj in range(1, 5):
                    s.stt(accf[:], vcb[:, base + 59 + jj:base + 571 + jj],
                          fir_s[:, j * 5 + jj: j * 5 + jj + 1], accf[:], ALU.mult, ALU.add)
                s.copy(dst_s, accf[:])
                s.ts(accf[:], vcb[:, base:base + 512], fir_l[:, j * 64:j * 64 + 1], None, ALU.mult)
                for jj in range(1, 64):
                    s.stt(accf[:], vcb[:, base + jj:base + jj + 512],
                          fir_l[:, j * 64 + jj:j * 64 + jj + 1], accf[:], ALU.mult, ALU.add)
                s.copy(dst_l, accf[:])
            for j in range(8):
                s.dma(xnTf[:, j * 512:(j + 1) * 512], xnT[j * 128:(j + 1) * 128, c0:c0 + 512])
                s.scopy(xnTb[:, j * 512:(j + 1) * 512], xnTf[:, j * 512:(j + 1) * 512])
            for j in range(8):
                for kk in range(8):
                    s.mm(bank2[:, 0:512], w1a_bf[:, kk * 1024 + j * 128: kk * 1024 + j * 128 + 128],
                         xnTb[:, kk * 512:(kk + 1) * 512], start=(kk == 0), stop=(kk == 7))
                s.copy(H1s[:, j * 512:(j + 1) * 512], bank2[:, 0:512])

            for sub in range(4):
                cc = bc0 + sub * 128
                tok0 = t0 + sub * 128
                for j in range(8):
                    s.trans(R_tr, vcb[:, j * 576 + 63 + sub * 128: j * 576 + 63 + sub * 128 + 128], ident[:])
                    s.copy(vd_tm[:, j * 128:(j + 1) * 128], R_tr)
                    s.trans(R_tr, shf[:, j * 512 + sub * 128: j * 512 + sub * 128 + 128], ident[:])
                    s.copy(sh_tm[:, j * 128:(j + 1) * 128], R_tr)
                    s.trans(R_tr, lof[:, j * 512 + sub * 128: j * 512 + sub * 128 + 128], ident[:])
                    s.copy(lo_tm[:, j * 128:(j + 1) * 128], R_tr)
                s.dma(dl_tm[:], delta[tok0:tok0 + 128, :])
                for (ix, X) in enumerate((sh_tm, lo_tm, dl_tm, vd_tm)):
                    X3 = X[:].rearrange("p (h d) -> p h d", h=4)
                    s.red(sc1[:], X3, ALU.add)
                    s.act(sq_scr[:], X[:], AF.Square)
                    s.red(sc2[:], sq_scr[:].rearrange("p (h d) -> p h d", h=4), ALU.add)
                    s.ts(stats_all[:].rearrange("p (h ss) -> p h ss", h=4)[:, :, ix * 4 + 0],
                         sc1[:], 1.0 / DH0, None, ALU.mult)
                    s.ts(sc1[:], sc1[:], 1.0 / DH0, None, ALU.mult)
                    s.act(sc3[:], sc1[:], AF.Square)
                    s.ts(sc1[:], sc2[:], 1.0 / DH0, None, ALU.mult)
                    s.tt(stats_all[:].rearrange("p (h ss) -> p h ss", h=4)[:, :, ix * 4 + 1],
                         sc1[:], sc3[:], ALU.subtract)
                    s.red(sc3[:], X3, ALU.add, absval=True)
                    s.ts(stats_all[:].rearrange("p (h ss) -> p h ss", h=4)[:, :, ix * 4 + 2],
                         sc3[:], 1.0 / DH0, None, ALU.mult)
                    s.act(stats_all[:].rearrange("p (h ss) -> p h ss", h=4)[:, :, ix * 4 + 3],
                          sc2[:], AF.Sqrt)
                for h in range(H0):
                    s.trans(bank1[0:16, 128:256], stats_all[:, h * 16:(h + 1) * 16], ident[:])
                    s.copy(statsT[:, h * 512 + sub * 128: h * 512 + sub * 128 + 128], bank1[0:16, 128:256])

                for h in range(H0):
                    for j in range(8):
                        s.mm(bank2[:, 0:128], w1b_s[:, j * 128:(j + 1) * 128],
                             statsT[:, h * 512 + sub * 128: h * 512 + sub * 128 + 128],
                             start=True, stop=True)
                        s.tt(sq_scr[:, 0:128], bank2[:, 0:128],
                             H1s[:, j * 512 + sub * 128: j * 512 + sub * 128 + 128], ALU.add)
                        s.emit("scalar", lambda e, o=hdn_bf, i=sq_scr, bcol=b1_s[:, j:j + 1]:
                               e.activation(o[:, 0:128], i[:, 0:128], AF.Gelu, bias=bcol), 1)
                        s.mm(R_l4[:, 0:128], w2_bf[:, j * 4:(j + 1) * 4], hdn_bf[:, 0:128],
                             start=(j == 0), stop=(j == 7))
                    s.ts(lg_s[:, 0:128], R_l4[:, 0:128], b2_s[:, 0:1], rtemp_b[0:4, 0:1],
                         ALU.add, ALU.mult)
                    s.trans(bank1[0:128, 128:132], lg_s[0:4, 0:128], ident[0:4, 0:4])
                    s.copy(fw_s[:, h * 4:(h + 1) * 4], bank1[0:128, 128:132])
                    s.red(sc1[:, 0:1], fw_s[:, h * 4:(h + 1) * 4], ALU.max, negate=True)
                    s.emit("scalar", lambda e, o=fw_s, hh=h, bcol=sc1[:, 0:1]:
                           e.activation(o[:, hh * 4:(hh + 1) * 4], o[:, hh * 4:(hh + 1) * 4],
                                        AF.Exp, bias=bcol), 1)
                    s.red(sc1[:, 0:1], fw_s[:, h * 4:(h + 1) * 4], ALU.add)
                    s.recip(sc1[:, 0:1], sc1[:, 0:1])
                    s.ts(fw_s[:, h * 4:(h + 1) * 4], fw_s[:, h * 4:(h + 1) * 4],
                         sc1[:, 0:1], None, ALU.mult)
                    s.tt(fw_s[:, h * 4 + 0:h * 4 + 1], fw_s[:, h * 4 + 0:h * 4 + 1], rss_b[:], ALU.add)
                    s.tt(fw_s[:, h * 4 + 1:h * 4 + 2], fw_s[:, h * 4 + 1:h * 4 + 2], rsl_b[:], ALU.add)
                    hs = slice(h * 256, (h + 1) * 256)
                    s.ts(o_tm[:, hs], sh_tm[:, hs], fw_s[:, h * 4 + 0:h * 4 + 1], None, ALU.mult)
                    s.stt(o_tm[:, hs], lo_tm[:, hs], fw_s[:, h * 4 + 1:h * 4 + 2], o_tm[:, hs], ALU.mult, ALU.add)
                    s.stt(o_tm[:, hs], dl_tm[:, hs], fw_s[:, h * 4 + 2:h * 4 + 3], o_tm[:, hs], ALU.mult, ALU.add)
                    s.stt(o_tm[:, hs], vd_tm[:, hs], fw_s[:, h * 4 + 3:h * 4 + 4], o_tm[:, hs], ALU.mult, ALU.add)
                    s.act(sq_scr[:, 0:256], o_tm[:, hs], AF.Square)
                    s.red(sc1[:, 0:1], sq_scr[:, 0:256], ALU.add)
                    s.ts(sc1[:, 0:1], sc1[:, 0:1], 1.0 / DH0, None, ALU.mult)
                    s.act(sc1[:, 0:1], sc1[:, 0:1], AF.Sqrt, bias=epsb)
                    s.recip(sc1[:, 0:1], sc1[:, 0:1])
                    s.ts(o_tm[:, hs], o_tm[:, hs], sc1[:, 0:1], None, ALU.mult)
                for j in range(8):
                    s.trans(R_tr, o_tm[:, j * 128:(j + 1) * 128], ident[:])
                    s.copy(oT_s[:, j * 512 + sub * 128: j * 512 + sub * 128 + 128], R_tr)
            for j in range(8):
                for kk in range(8):
                    s.mm(bank2[:, 0:512], opT_s[:, kk * 1024 + j * 128: kk * 1024 + j * 128 + 128],
                         oT_s[:, kk * 512:(kk + 1) * 512], start=(kk == 0), stop=(kk == 7))
                for sub in range(4):
                    s.copy(stg[:, 0:128], bank2[:, sub * 128:sub * 128 + 128])
                    s.trans(R_tr, stg[:, 0:128], ident[:])
                    s.copy(o_tm[:, j * 128:(j + 1) * 128], R_tr)
                    s.dma(attn_o[t0 + sub * 128:t0 + sub * 128 + 128, j * 128:(j + 1) * 128],
                          o_tm[:, j * 128:(j + 1) * 128])
        s.finalize(nc)
    print("p3 instrs:", len(s.q), "sem total:", s.cnt)
    return nc


def run_p3(inputs, r1_results, delta_list):
    consts = make_consts()
    fw1 = inputs["L0_attn_fuse_w1"]
    firs = inputs["L0_attn_fir_short_w"].reshape(1024, 5)
    firl = inputs["L0_attn_fir_long_w"].reshape(1024, 64)
    firs_t = np.ascontiguousarray(firs.reshape(8, 128, 5).transpose(1, 0, 2).reshape(128, 40))
    firl_t = np.ascontiguousarray(firl.reshape(8, 128, 64).transpose(1, 0, 2).reshape(128, 512))
    base = {
        "firs": firs_t,
        "firl": firl_t,
        "w1aT": np.ascontiguousarray(fw1[:, :1024].T),
        "w1bT": np.ascontiguousarray(fw1[:, 1024:].T),
        "b1c": np.ascontiguousarray(inputs["L0_attn_fuse_b1"].reshape(8, 128).T),
        "w2T": np.ascontiguousarray(inputs["L0_attn_fuse_w2"].T),
        "b2c": inputs["L0_attn_fuse_b2"].reshape(4, 1),
        "ltemp": inputs["L0_attn_logit_temp"].reshape(1, 1),
        "rss": inputs["L0_attn_res_scale_short"].reshape(1, 1),
        "rsl": inputs["L0_attn_res_scale_long"].reshape(1, 1),
        "opT": np.ascontiguousarray(inputs["L0_attn_o_proj_w"].T),
        "onwc": np.ascontiguousarray(np.tile(inputs["L0_attn_o_norm_w"], 4).reshape(8, 128).T),
        "c_ident": consts["ident"],
        "c_ones_row": consts["ones_row"],
        "c_ccols": consts["ccols"],
    }
    attn = [np.zeros((TPC, D), np.float32) for _ in range(NCORES)]
    for b in range(BPC):
        nc = build_p3(b)
        in_maps = []
        for c in range(NCORES):
            m = dict(base)
            m["vc"] = r1_results[c]["vc"]
            m["delta"] = delta_list[c]
            m["xnT"] = r1_results[c]["xnT"]
            in_maps.append(m)
        res = run_bass_kernel_spmd(nc, in_maps, core_ids=list(range(NCORES)))
        for c in range(NCORES):
            attn[c][b * 2048:(b + 1) * 2048] = res.results[c]["attn0"][b * 2048:(b + 1) * 2048]
    return attn


def build_p3b():
    nc = mk_nc()
    s = Ser()
    x = nc.dram_tensor("x", [TPC, D], F32, kind="ExternalInput").ap()
    xn = nc.dram_tensor("xn", [TPC, D], F32, kind="ExternalInput").ap()
    attn0 = nc.dram_tensor("attn0", [TPC, D], F32, kind="ExternalInput").ap()
    wgT = nc.dram_tensor("wgT", [D, 4096], F32, kind="ExternalInput").ap()
    wuT = nc.dram_tensor("wuT", [D, 4096], F32, kind="ExternalInput").ap()
    wdT = nc.dram_tensor("wdT", [4096, D], F32, kind="ExternalInput").ap()
    wrT = nc.dram_tensor("wrT", [D, 8], F32, kind="ExternalInput").ap()
    brc = nc.dram_tensor("brc", [8, 1], F32, kind="ExternalInput").ap()
    x1_o = nc.dram_tensor("x1", [TPC, D], F32, kind="ExternalOutput").ap()
    xn1_o = nc.dram_tensor("xn1", [TPC, D], F32, kind="ExternalOutput").ap()
    xn1T_o = nc.dram_tensor("xn1T", [D, TPC], F32, kind="ExternalOutput").ap()
    probs_o = nc.dram_tensor("probs", [TPC, 8], F32, kind="ExternalOutput").ap()
    g_d = nc.dram_tensor("g_d", [TPC, 4096], F32, kind="Internal").ap()
    xqT_d = nc.dram_tensor("xqT_d", [D, TPC], BF16, kind="Internal").ap()
    zq_d = nc.dram_tensor("zq_d", [TPC, 4096], BF16, kind="Internal").ap()
    dqx_d = nc.dram_tensor("dqx_d", [TPC, 1], F32, kind="Internal").ap()
    dqz_d = nc.dram_tensor("dqz_d", [TPC, 1], F32, kind="Internal").ap()

    with contextlib.ExitStack() as st:
        ident = sb(nc, st, "ident", [128, 128])
        ones_row = sb(nc, st, "ones_row", [1, 128])
        ones_col = sb(nc, st, "ones_col", [128, 1])
        ccols = sb(nc, st, "ccols", [128, 5])
        s.dma(ident[:], nc.dram_tensor("c_ident", [128, 128], F32, kind="ExternalInput").ap())
        s.dma(ones_row[:], nc.dram_tensor("c_ones_row", [1, 128], F32, kind="ExternalInput").ap())
        s.dma(ones_col[:], nc.dram_tensor("c_ones_col", [128, 1], F32, kind="ExternalInput").ap())
        s.dma(ccols[:], nc.dram_tensor("c_ccols", [128, 5], F32, kind="ExternalInput").ap())
        epsb = ccols[:, 0:1]

        ident_bf = sb(nc, st, "ident_bf", [128, 128], BF16)
        s.scopy(ident_bf[:], ident[:])
        wbuf = sb(nc, st, "wbuf", [128, 32768], BF16)
        zscr = sb(nc, st, "zscr", [128, 4096])
        zscr2 = sb(nc, st, "zscr2", [128, 4096])
        z_tm = sb(nc, st, "z_tm", [128, 4096])
        zq_bf = sb(nc, st, "zq_bf", [128, 4096], BF16)
        zqT_bf = sb(nc, st, "zqT_bf", [128, 4096], BF16)
        xt = sb(nc, st, "xt", [128, D])
        xnt = sb(nc, st, "xnt", [128, D])
        xq_bf = sb(nc, st, "xq_bf", [128, D], BF16)
        xqT_bf = sb(nc, st, "xqT_bf", [128, 8 * 512], BF16)
        macc = sb(nc, st, "macc", [128, 1])
        mg_b = sb(nc, st, "mg_b", [128, 1])
        mu_b = sb(nc, st, "mu_b", [128, 1])
        md_b = sb(nc, st, "md_b", [128, 1])
        sc_a = sb(nc, st, "sc_a", [128, 1])
        sc_b = sb(nc, st, "sc_b", [128, 1])
        dqx = sb(nc, st, "dqx", [128, 1])
        dqz = sb(nc, st, "dqz", [128, 1])
        g_s = sb(nc, st, "g_s", [128, 512])
        u_s = sb(nc, st, "u_s", [128, 512])
        wr_s = sb(nc, st, "wr_s", [128, 64])
        br_s = sb(nc, st, "br_s", [8, 1])
        pr_s = sb(nc, st, "pr_s", [8, 512])
        pb_s = sb(nc, st, "pb_s", [128, 8])
        xnT_blk = sb(nc, st, "xnT_blk", [128, 8 * 512])
        st1 = sb(nc, st, "st1", [128, 1])
        st2 = sb(nc, st, "st2", [128, 1])
        st3 = sb(nc, st, "st3", [128, 1])
        sone = sb(nc, st, "sone", [1, 1])

        bankA = ps(nc, st, "bankA", [128, 512])
        bankB = ps(nc, st, "bankB", [128, 512])
        bankT = ps(nc, st, "bankT", [128, 256], BF16)
        bankS = ps(nc, st, "bankS", [128, 512])
        R_one = bankS[0:1, 0:1]

        def wquant(srcT, n_in_tiles, out_w, m_bcast):
            for j in range(n_in_tiles):
                s.dma(zscr[:, 0:out_w], srcT[j * 128:(j + 1) * 128, :])
                s.red(sc_a[:], zscr[:, 0:out_w], ALU.add, absval=True)
                if j == 0:
                    s.copy(macc[:], sc_a[:])
                else:
                    s.tt(macc[:], macc[:], sc_a[:], ALU.add)
            s.mm(R_one, ones_col[:], macc[:], start=True, stop=True)
            s.ts(sone[:], R_one, 1.0 / (n_in_tiles * 128 * out_w), 1e-5, ALU.mult, ALU.max)
            s.mm(bankS[:, 1:2], ones_row[:], sone[:], start=True, stop=True)
            s.copy(m_bcast[:], bankS[:, 1:2])
            s.recip(sc_b[:], m_bcast[:])
            for j in range(n_in_tiles):
                s.dma(zscr[:, 0:out_w], srcT[j * 128:(j + 1) * 128, :])
                s.emit("scalar", lambda e, o=zscr2, i=zscr, w=out_w, sc=sc_b:
                       e.activation(o[:, 0:w], i[:, 0:w], AF.Abs, scale=sc[:, 0:1]), 1)
                s.ts(z_tm[:, 0:out_w], zscr2[:, 0:out_w], 0.5, None, ALU.is_ge)
                s.act(zscr2[:, 0:out_w], zscr[:, 0:out_w], AF.Sign)
                s.tt(zscr[:, 0:out_w], z_tm[:, 0:out_w], zscr2[:, 0:out_w], ALU.mult)
                s.scopy(wbuf[:, j * out_w:(j + 1) * out_w], zscr[:, 0:out_w])

        def actquant(src_t, width, dq_col, out_bf, ge_scr):
            s.red(sc_a[:], src_t, ALU.max, absval=True)
            s.ts(sc_a[:], sc_a[:], 1e-5, None, ALU.max)
            s.ts(dq_col[:], sc_a[:], 1.0 / 127.0, None, ALU.mult)
            s.recip(sc_b[:], sc_a[:])
            s.ts(sc_b[:], sc_b[:], 127.0, None, ALU.mult)
            s.emit("scalar", lambda e, o=zscr, i=src_t, w=width, sc=sc_b:
                   e.activation(o[:, 0:w], i, AF.Identity, bias=ccols[:, 4:5], scale=sc[:, 0:1]), 1)
            s.copy(zscr2[:, 0:width], zscr[:, 0:width])
            for bit in (128.0, 64.0, 32.0, 16.0, 8.0, 4.0, 2.0, 1.0):
                s.ts(ge_scr[:, 0:width], zscr2[:, 0:width], bit, None, ALU.is_ge)
                s.stt(zscr2[:, 0:width], ge_scr[:, 0:width], -bit, zscr2[:, 0:width], ALU.mult, ALU.add)
            s.tt(zscr[:, 0:width], zscr[:, 0:width], zscr2[:, 0:width], ALU.subtract)
            s.ts(zscr[:, 0:width], zscr[:, 0:width], -128.0, None, ALU.add)
            s.scopy(out_bf, zscr[:, 0:width])

        wquant(wgT, 8, 4096, mg_b)
        for kk in range(8):
            s.dma(wr_s[:, kk * 8:(kk + 1) * 8], wrT[kk * 128:(kk + 1) * 128, :])
        s.dma(br_s[:], brc)
        for blk in range(8):
            t0 = blk * 512
            for sub in range(4):
                r0 = t0 + sub * 128
                s.dma(xnt[:], xn[r0:r0 + 128, :])
                actquant(xnt[:], D, dqx, xq_bf[:], xnt)
                s.dma(dqx_d[r0:r0 + 128, :], dqx[:])
                for j in range(8):
                    s.trans(bankT[:, 0:128], xq_bf[:, j * 128:(j + 1) * 128], ident_bf[:])
                    s.copy(xqT_bf[:, j * 512 + sub * 128: j * 512 + sub * 128 + 128], bankT[:, 0:128])
                s.tt(sc_a[:], dqx[:], mg_b[:], ALU.mult)
                for ns in range(8):
                    for kk in range(8):
                        s.mm(bankA[:, 0:512], xqT_bf[:, kk * 512 + sub * 128: kk * 512 + sub * 128 + 128],
                             wbuf[:, kk * 4096 + ns * 512: kk * 4096 + (ns + 1) * 512],
                             start=(kk == 0), stop=(kk == 7))
                    s.ts(g_s[:], bankA[:, 0:512], sc_a[:, 0:1], None, ALU.mult)
                    s.act(z_tm[:, ns * 512:(ns + 1) * 512], g_s[:], AF.Silu)
                s.dma(g_d[r0:r0 + 128, :], z_tm[:])
            for j in range(8):
                s.dma(xqT_d[j * 128:(j + 1) * 128, t0:t0 + 512], xqT_bf[:, j * 512:(j + 1) * 512])

        wquant(wuT, 8, 4096, mu_b)
        for blk in range(8):
            t0 = blk * 512
            for j in range(8):
                s.dma(xqT_bf[:, j * 512:(j + 1) * 512], xqT_d[j * 128:(j + 1) * 128, t0:t0 + 512])
            for sub in range(4):
                r0 = t0 + sub * 128
                s.dma(dqx[:], dqx_d[r0:r0 + 128, :])
                s.dma(z_tm[:], g_d[r0:r0 + 128, :])
                s.tt(sc_a[:], dqx[:], mu_b[:], ALU.mult)
                for ns in range(8):
                    for kk in range(8):
                        s.mm(bankA[:, 0:512], xqT_bf[:, kk * 512 + sub * 128: kk * 512 + sub * 128 + 128],
                             wbuf[:, kk * 4096 + ns * 512: kk * 4096 + (ns + 1) * 512],
                             start=(kk == 0), stop=(kk == 7))
                    s.ts(u_s[:], bankA[:, 0:512], sc_a[:, 0:1], None, ALU.mult)
                    s.tt(z_tm[:, ns * 512:(ns + 1) * 512], z_tm[:, ns * 512:(ns + 1) * 512],
                         u_s[:], ALU.mult)
                actquant(z_tm[:], 4096, dqz, zq_bf[:], z_tm)
                s.dma(zq_d[r0:r0 + 128, :], zq_bf[:])
                s.dma(dqz_d[r0:r0 + 128, :], dqz[:])

        wquant(wdT, 32, 1024, md_b)
        for blk in range(8):
            t0 = blk * 512
            for sub in range(4):
                r0 = t0 + sub * 128
                s.dma(zq_bf[:], zq_d[r0:r0 + 128, :])
                s.dma(dqz[:], dqz_d[r0:r0 + 128, :])
                for j in range(32):
                    s.trans(bankT[:, 0:128], zq_bf[:, j * 128:(j + 1) * 128], ident_bf[:])
                    s.copy(zqT_bf[:, j * 128:(j + 1) * 128], bankT[:, 0:128])
                s.tt(sc_a[:], dqz[:], md_b[:], ALU.mult)
                s.dma(xt[:], x[r0:r0 + 128, :])
                s.dma(xnt[:], attn0[r0:r0 + 128, :])
                s.tt(xt[:], xt[:], xnt[:], ALU.add)
                for half in range(2):
                    for kk in range(32):
                        s.mm(bankA[:, 0:512], zqT_bf[:, kk * 128:(kk + 1) * 128],
                             wbuf[:, kk * 1024 + half * 512: kk * 1024 + half * 512 + 512],
                             start=(kk == 0), stop=(kk == 31))
                    s.ts(g_s[:], bankA[:, 0:512], sc_a[:, 0:1], None, ALU.mult)
                    s.tt(xt[:, half * 512:(half + 1) * 512], xt[:, half * 512:(half + 1) * 512],
                         g_s[:], ALU.add)
                s.dma(x1_o[r0:r0 + 128, :], xt[:])
                s.red(st1[:], xt[:], ALU.add)
                s.ttr(st2[:], xt[:], xt[:], zscr[:, 0:1024])
                s.ts(st1[:], st1[:], 1.0 / D, None, ALU.mult)
                s.ts(st2[:], st2[:], 1.0 / D, None, ALU.mult)
                s.emit("scalar", lambda e, o=st3, i=st1: e.square(o[:], i[:]), 1)
                s.tt(st2[:], st2[:], st3[:], ALU.subtract)
                s.act(st3[:], st2[:], AF.Sqrt, bias=epsb)
                s.recip(st3[:], st3[:])
                s.emit("vector", lambda e, o=xnt, i=xt, m=st1, r=st3: e.tensor_scalar(
                    o[:], i[:], m[:], r[:], ALU.subtract, ALU.mult), 1)
                s.dma(xn1_o[r0:r0 + 128, :], xnt[:])
                for j in range(8):
                    s.trans(bankS[:, 0:128], xnt[:, j * 128:(j + 1) * 128], ident[:])
                    s.copy(xnT_blk[:, j * 512 + sub * 128: j * 512 + sub * 128 + 128], bankS[:, 0:128])
            for j in range(8):
                s.dma(xn1T_o[j * 128:(j + 1) * 128, t0:t0 + 512], xnT_blk[:, j * 512:(j + 1) * 512])
            for kk in range(8):
                s.mm(bankA[0:8, 0:512], wr_s[:, kk * 8:(kk + 1) * 8],
                     xnT_blk[:, kk * 512:(kk + 1) * 512], start=(kk == 0), stop=(kk == 7))
            s.ts(pr_s[:], bankA[0:8, 0:512], br_s[:, 0:1], None, ALU.add)
            for sub in range(4):
                s.trans(bankS[0:128, 0:8], pr_s[:, sub * 128:(sub + 1) * 128], ident[0:8, 0:8])
                s.copy(pb_s[:], bankS[0:128, 0:8])
                s.red(st1[:], pb_s[:], ALU.max, negate=True)
                s.emit("scalar", lambda e, o=pb_s, bcol=st1: e.activation(
                    o[:], o[:], AF.Exp, bias=bcol[:, 0:1]), 1)
                s.red(st1[:], pb_s[:], ALU.add)
                s.recip(st1[:], st1[:])
                s.ts(pb_s[:], pb_s[:], st1[:, 0:1], None, ALU.mult)
                s.dma(probs_o[t0 + sub * 128: t0 + sub * 128 + 128, :], pb_s[:])
        s.finalize(nc)
    print("p3b instrs:", len(s.q), "sem total:", s.cnt)
    return nc


def run_p3b(inputs, xfull, r1_results, attn_list):
    consts = make_consts()
    nc = build_p3b()
    base = {
        "wgT": np.ascontiguousarray(inputs["L0_ffn_w_gate"].T),
        "wuT": np.ascontiguousarray(inputs["L0_ffn_w_up"].T),
        "wdT": np.ascontiguousarray(inputs["L0_ffn_w_down"].T),
        "wrT": np.ascontiguousarray(inputs["L1_ffn_W_router"].T),
        "brc": inputs["L1_ffn_b_router"].reshape(8, 1),
        "c_ident": consts["ident"],
        "c_ones_row": consts["ones_row"],
        "c_ones_col": consts["ones_col"],
        "c_ccols": consts["ccols"],
    }
    in_maps = []
    for c in range(NCORES):
        m = dict(base)
        m["x"] = np.ascontiguousarray(xfull[c * TPC:(c + 1) * TPC])
        m["xn"] = r1_results[c]["xn"]
        m["attn0"] = attn_list[c]
        in_maps.append(m)
    res = run_bass_kernel_spmd(nc, in_maps, core_ids=list(range(NCORES)))
    return res.results


def build_p4(b):
    nc = mk_nc()
    s = Ser()
    xn1T = nc.dram_tensor("xn1T", [D, TPC], F32, kind="ExternalInput").ap()
    Wqa = nc.dram_tensor("Wqa", [D, 64], F32, kind="ExternalInput").ap()
    Wqb = nc.dram_tensor("Wqb", [64, D], F32, kind="ExternalInput").ap()
    Wkva = nc.dram_tensor("Wkva", [D, 32], F32, kind="ExternalInput").ap()
    Wkvb = nc.dram_tensor("Wkvb", [32, 512], F32, kind="ExternalInput").ap()
    WoutT = nc.dram_tensor("WoutT", [D, D], F32, kind="ExternalInput").ap()
    boutc = nc.dram_tensor("boutc", [128, 8], F32, kind="ExternalInput").ap()
    cqh = nc.dram_tensor("cqh", [SEQ, D], F32, kind="ExternalInput").ap()
    cql = nc.dram_tensor("cql", [SEQ, D], F32, kind="ExternalInput").ap()
    sqh = nc.dram_tensor("sqh", [SEQ, D], F32, kind="ExternalInput").ap()
    sql = nc.dram_tensor("sql", [SEQ, D], F32, kind="ExternalInput").ap()
    ckh = nc.dram_tensor("ckh", [SEQ, 256], F32, kind="ExternalInput").ap()
    ckl = nc.dram_tensor("ckl", [SEQ, 256], F32, kind="ExternalInput").ap()
    skh = nc.dram_tensor("skh", [SEQ, 256], F32, kind="ExternalInput").ap()
    skl = nc.dram_tensor("skl", [SEQ, 256], F32, kind="ExternalInput").ap()
    attn_o = nc.dram_tensor("attn1", [TPC, D], F32, kind="ExternalOutput").ap()

    with contextlib.ExitStack() as st:
        ident = sb(nc, st, "ident", [128, 128])
        causal = sb(nc, st, "causal", [128, 128])
        s.dma(ident[:], nc.dram_tensor("c_ident", [128, 128], F32, kind="ExternalInput").ap())
        s.dma(causal[:], nc.dram_tensor("c_causal_add", [128, 128], F32, kind="ExternalInput").ap())
        wqa_s = sb(nc, st, "wqa_s", [128, 8 * 64])
        wqb_s = sb(nc, st, "wqb_s", [64, D])
        wkva_s = sb(nc, st, "wkva_s", [128, 8 * 32])
        wkvb_s = sb(nc, st, "wkvb_s", [32, 512])
        wout_s = sb(nc, st, "wout_s", [128, 8 * 1024], BF16)
        wstg = sb(nc, st, "wstg", [128, 1024])
        bout_s = sb(nc, st, "bout_s", [128, 8])
        qmidT = sb(nc, st, "qmidT", [64, SEQ])
        kvmidT = sb(nc, st, "kvmidT", [32, SEQ])
        ident_bf = sb(nc, st, "ident_bf", [128, 128], BF16)
        s.scopy(ident_bf[:], ident[:])
        qT = sb(nc, st, "qT", [64, 16 * SEQ], BF16)
        kT = sb(nc, st, "kT", [64, 4 * SEQ], BF16)
        V = sb(nc, st, "V", [128, 16 * 256], BF16)
        q_bf = sb(nc, st, "q_bf", [128, D], BF16)
        p_bf = sb(nc, st, "p_bf", [128, 128], BF16)
        pT_bf = sb(nc, st, "pT_bf", [128, 128], BF16)
        o_bf = sb(nc, st, "o_bf", [128, D], BF16)
        q_tm = sb(nc, st, "q_tm", [128, D])
        qsw = sb(nc, st, "qsw", [128, D])
        ctab = sb(nc, st, "ctab", [128, D])
        stab = sb(nc, st, "stab", [128, D])
        kv_tm = sb(nc, st, "kv_tm", [128, 512])
        ksw = sb(nc, st, "ksw", [128, 256])
        scores = sb(nc, st, "scores", [128, SEQ])
        out_tm = sb(nc, st, "out_tm", [128, D])
        outT = sb(nc, st, "outT", [128, 8 * 512], BF16)
        o_st = sb(nc, st, "o_st", [128, 512])
        nmax = sb(nc, st, "nmax", [128, 1])
        rsum = sb(nc, st, "rsum", [128, 1])

        bankA = ps(nc, st, "bankA", [128, 512])
        bankB = ps(nc, st, "bankB", [128, 512])
        bankC = ps(nc, st, "bankC", [128, 512])
        bankT = ps(nc, st, "bankT", [128, 256], BF16)
        R_tr = bankC[:, 0:128]
        R_o = bankB[:, 0:64]

        c0 = b * SEQ
        for kk in range(8):
            s.dma(wqa_s[:, kk * 64:(kk + 1) * 64], Wqa[kk * 128:(kk + 1) * 128, :])
            s.dma(wkva_s[:, kk * 32:(kk + 1) * 32], Wkva[kk * 128:(kk + 1) * 128, :])
            s.dma(wstg[:], WoutT[kk * 128:(kk + 1) * 128, :])
            s.scopy(wout_s[:, kk * 1024:(kk + 1) * 1024], wstg[:])
        s.dma(wqb_s[:], Wqb)
        s.dma(wkvb_s[:], Wkvb)
        s.dma(bout_s[:], boutc)

        xnT_blk = sb(nc, st, "xnT_blk", [128, 8 * 512])
        for blk in range(4):
            t0 = blk * 512
            for kk in range(8):
                s.dma(xnT_blk[:, kk * 512:(kk + 1) * 512],
                      xn1T[kk * 128:(kk + 1) * 128, c0 + t0:c0 + t0 + 512])
            for kk in range(8):
                s.mm(bankA[0:64, 0:512], wqa_s[:, kk * 64:(kk + 1) * 64],
                     xnT_blk[:, kk * 512:(kk + 1) * 512],
                     start=(kk == 0), stop=(kk == 7))
            s.copy(qmidT[:, t0:t0 + 512], bankA[0:64, 0:512])
            for kk in range(8):
                s.mm(bankB[0:32, 0:512], wkva_s[:, kk * 32:(kk + 1) * 32],
                     xnT_blk[:, kk * 512:(kk + 1) * 512],
                     start=(kk == 0), stop=(kk == 7))
            s.copy(kvmidT[:, t0:t0 + 512], bankB[0:32, 0:512])

        for ti in range(16):
            t0 = ti * 128
            for half in range(2):
                s.mm(bankA[:, 0:512], qmidT[:, t0:t0 + 128],
                     wqb_s[:, half * 512:(half + 1) * 512], start=True, stop=True)
                s.copy(q_tm[:, half * 512:(half + 1) * 512], bankA[:, 0:512])
            s.dma(ctab[:], cqh[t0:t0 + 128, :])
            s.dma(stab[:], cql[t0:t0 + 128, :])
            s.tt(ctab[:], ctab[:], stab[:], ALU.add)
            s.dma(stab[:], sqh[t0:t0 + 128, :])
            s.dma(qsw[:], sql[t0:t0 + 128, :])
            s.tt(stab[:], stab[:], qsw[:], ALU.add)
            q3 = q_tm[:].rearrange("p (f two) -> p f two", two=2)
            w3 = qsw[:].rearrange("p (f two) -> p f two", two=2)
            s.copy(w3[:, :, 0], q3[:, :, 1])
            s.copy(w3[:, :, 1], q3[:, :, 0])
            s.tt(q_tm[:], q_tm[:], ctab[:], ALU.mult)
            s.tt(qsw[:], qsw[:], stab[:], ALU.mult)
            s.tt(q_tm[:], q_tm[:], qsw[:], ALU.add)
            s.scopy(q_bf[:], q_tm[:])
            for j in range(8):
                s.trans(bankT[:, 0:128], q_bf[:, j * 128:(j + 1) * 128], ident_bf[:])
                s.copy(qT[:, (2 * j) * SEQ + t0: (2 * j) * SEQ + t0 + 128], bankT[0:64, 0:128])
                s.copy(qT[:, (2 * j + 1) * SEQ + t0: (2 * j + 1) * SEQ + t0 + 128],
                       bankT[64:128, 0:128])
            s.mm(bankA[:, 0:512], kvmidT[:, t0:t0 + 128], wkvb_s[:], start=True, stop=True)
            s.copy(kv_tm[:], bankA[:, 0:512])
            s.dma(ctab[:, 0:256], ckh[t0:t0 + 128, :])
            s.dma(stab[:, 0:256], ckl[t0:t0 + 128, :])
            s.tt(ctab[:, 0:256], ctab[:, 0:256], stab[:, 0:256], ALU.add)
            s.dma(stab[:, 0:256], skh[t0:t0 + 128, :])
            s.dma(ksw[:], skl[t0:t0 + 128, :])
            s.tt(stab[:, 0:256], stab[:, 0:256], ksw[:], ALU.add)
            k3 = kv_tm[:, 0:256].rearrange("p (f two) -> p f two", two=2)
            ks3 = ksw[:].rearrange("p (f two) -> p f two", two=2)
            s.copy(ks3[:, :, 0], k3[:, :, 1])
            s.copy(ks3[:, :, 1], k3[:, :, 0])
            s.tt(kv_tm[:, 0:256], kv_tm[:, 0:256], ctab[:, 0:256], ALU.mult)
            s.tt(ksw[:], ksw[:], stab[:, 0:256], ALU.mult)
            s.tt(kv_tm[:, 0:256], kv_tm[:, 0:256], ksw[:], ALU.add)
            s.scopy(q_bf[:, 0:256], kv_tm[:, 0:256])
            for half in range(2):
                s.trans(bankT[:, 0:128], q_bf[:, half * 128:half * 128 + 128], ident_bf[:])
                s.copy(kT[:, (2 * half) * SEQ + t0: (2 * half) * SEQ + t0 + 128], bankT[0:64, 0:128])
                s.copy(kT[:, (2 * half + 1) * SEQ + t0: (2 * half + 1) * SEQ + t0 + 128],
                       bankT[64:128, 0:128])
            s.scopy(V[:, ti * 256:(ti + 1) * 256], kv_tm[:, 256:512])

        for qi in range(16):
            nb = (qi * 128 + 128 + 511) // 512
            L = nb * 512
            for h in range(16):
                kvh = h // 4
                for kb in range(nb):
                    s.mm(bankA[:, 0:512],
                         qT[:, h * SEQ + qi * 128: h * SEQ + qi * 128 + 128],
                         kT[:, kvh * SEQ + kb * 512: kvh * SEQ + (kb + 1) * 512],
                         start=True, stop=True)
                    s.act(scores[:, kb * 512:(kb + 1) * 512], bankA[:, 0:512],
                          AF.Identity, scale=0.125)
                s.tt(scores[:, qi * 128:qi * 128 + 128], scores[:, qi * 128:qi * 128 + 128],
                     causal[:], ALU.add)
                if qi * 128 + 128 < L:
                    s.memset(scores[:, qi * 128 + 128:L], -1e30)
                s.red(nmax[:], scores[:, 0:L], ALU.max, negate=True)
                s.emit("scalar", lambda e, o=scores, LL=L, bcol=nmax: e.activation(
                    o[:, 0:LL], o[:, 0:LL], AF.Exp, bias=bcol[:, 0:1]), 1)
                s.red(rsum[:], scores[:, 0:L], ALU.add)
                s.recip(rsum[:], rsum[:])
                for kt in range(qi + 1):
                    s.scopy(p_bf[:], scores[:, kt * 128:(kt + 1) * 128])
                    s.trans(bankT[:, 0:128], p_bf[:], ident_bf[:])
                    s.copy(pT_bf[:], bankT[:, 0:128])
                    s.mm(R_o, pT_bf[:], V[:, kt * 256 + kvh * 64: kt * 256 + kvh * 64 + 64],
                         start=(kt == 0), stop=(kt == qi))
                s.ts(out_tm[:, h * 64:(h + 1) * 64], R_o, rsum[:, 0:1], None, ALU.mult)
            s.scopy(o_bf[:], out_tm[:])
            for j in range(8):
                s.trans(bankT[:, 0:128], o_bf[:, j * 128:(j + 1) * 128], ident_bf[:])
                s.copy(outT[:, j * 512 + (qi % 4) * 128: j * 512 + (qi % 4) * 128 + 128], bankT[:, 0:128])
            if qi % 4 == 3:
                blk0 = (qi // 4) * 512
                for j in range(8):
                    for kk in range(8):
                        s.mm(bankB[:, 0:512], wout_s[:, kk * 1024 + j * 128: kk * 1024 + j * 128 + 128],
                             outT[:, kk * 512:(kk + 1) * 512], start=(kk == 0), stop=(kk == 7))
                    s.ts(o_st[:], bankB[:, 0:512], bout_s[:, j:j + 1], None, ALU.add)
                    for sub in range(4):
                        s.trans(R_tr, o_st[:, sub * 128:sub * 128 + 128], ident[:])
                        s.copy(q_tm[:, j * 128:(j + 1) * 128], R_tr)
                        s.dma(attn_o[c0 + blk0 + sub * 128: c0 + blk0 + sub * 128 + 128,
                                     j * 128:(j + 1) * 128], q_tm[:, j * 128:(j + 1) * 128])
        s.finalize(nc)
    print("p4 instrs:", len(s.q), "sem total:", s.cnt)
    return nc


def _rope_tables(n, d_head, reps):
    inv = 1.0 / (10000.0 ** (np.arange(0, d_head, 2, dtype=np.float64) / d_head))
    t = np.arange(n, dtype=np.float64)
    fr = np.outer(t, inv)
    cos = np.cos(fr); sin = np.sin(fr)
    def expand(tab, sign_first):
        out = np.zeros((n, d_head), np.float64)
        out[:, 0::2] = tab * (-1.0 if sign_first else 1.0)
        out[:, 1::2] = tab
        return np.tile(out, (1, reps))
    cI = expand(cos, False)
    sI = expand(sin, True)
    def ds(a):
        hi = a.astype(np.float32)
        lo = (a - hi.astype(np.float64)).astype(np.float32)
        return hi, lo
    ch, cl = ds(cI); sh, sl = ds(sI)
    return ch, cl, sh, sl


def run_p4(inputs, r3b_results):
    consts = make_consts()
    cqh, cql, sqh, sql = _rope_tables(SEQ, 64, 16)
    ckh, ckl, skh, skl = _rope_tables(SEQ, 64, 4)
    kvb = inputs["L1_attn_Wkvb"]
    perm = np.array([kvh * 128 + d * 2 + j for j in range(2) for kvh in range(4) for d in range(64)])
    base = {
        "Wqa": inputs["L1_attn_Wqa"],
        "Wqb": inputs["L1_attn_Wqb"],
        "Wkva": inputs["L1_attn_Wkva"],
        "Wkvb": np.ascontiguousarray(kvb[:, perm]),
        "WoutT": np.ascontiguousarray(inputs["L1_attn_Wout"]),
        "boutc": np.ascontiguousarray(inputs["L1_attn_bout"].reshape(8, 128).T),
        "cqh": cqh, "cql": cql, "sqh": sqh, "sql": sql,
        "ckh": ckh, "ckl": ckl, "skh": skh, "skl": skl,
        "c_ident": consts["ident"],
        "c_causal_add": consts["causal_add"],
    }
    attn = [np.zeros((TPC, D), np.float32) for _ in range(NCORES)]
    for b in range(BPC):
        nc = build_p4(b)
        in_maps = []
        for c in range(NCORES):
            m = dict(base)
            m["xn1T"] = r3b_results[c]["xn1T"]
            in_maps.append(m)
        res = run_bass_kernel_spmd(nc, in_maps, core_ids=list(range(NCORES)))
        for c in range(NCORES):
            attn[c][b * SEQ:(b + 1) * SEQ] = res.results[c]["attn1"][b * SEQ:(b + 1) * SEQ]
    return attn


MOE_CAP = 5120
MOE_PAD = NTOK + MOE_CAP + 128

def build_p5(e):
    nc = mk_nc()
    s = Ser()
    xn1f = nc.dram_tensor("xn1f", [NTOK, D], F32, kind="ExternalInput").ap()
    probsf = nc.dram_tensor("probsf", [NTOK, 8], F32, kind="ExternalInput").ap()
    w1 = nc.dram_tensor("w1", [D, 4096], F32, kind="ExternalInput").ap()
    w2 = nc.dram_tensor("w2", [4096, D], F32, kind="ExternalInput").ap()
    tvals = nc.dram_tensor("tvals", [128, 256], U32, kind="ExternalInput").ap()
    tvals_f = nc.dram_tensor("tvals_f", [128, 256], F32, kind="ExternalInput").ap()
    prefill = nc.dram_tensor("prefill", [128, 40], U32, kind="ExternalInput").ap()
    su128 = nc.dram_tensor("c_su_mask", [128, 128], F32, kind="ExternalInput").ap()
    moe_o = nc.dram_tensor("moe", [MOE_PAD, D], F32, kind="ExternalOutput").ap()
    idxbuf = nc.dram_tensor("idxbuf", [MOE_PAD, 1], U32, kind="Internal").ap()
    gatebuf = nc.dram_tensor("gatebuf", [MOE_PAD, 1], F32, kind="Internal").ap()

    with contextlib.ExitStack() as st:
        ident = sb(nc, st, "ident", [128, 128])
        ident_bf = sb(nc, st, "ident_bf", [128, 128], BF16)
        s.dma(ident[:], nc.dram_tensor("c_ident", [128, 128], F32, kind="ExternalInput").ap())
        s.scopy(ident_bf[:], ident[:])
        su_s = sb(nc, st, "su_s", [128, 128])
        s.dma(su_s[:], su128)
        P_s = sb(nc, st, "P_s", [128, 2048])
        m1 = sb(nc, st, "m1", [128, 256])
        m2 = sb(nc, st, "m2", [128, 256])
        w2_s = sb(nc, st, "w2scr", [128, 2048])
        eqt = sb(nc, st, "eqt", [128, 256])
        tmp = sb(nc, st, "tmp", [128, 256])
        mask = sb(nc, st, "mask", [128, 256])
        gate = sb(nc, st, "gate", [128, 256])
        zeros = sb(nc, st, "zeros", [128, 256])
        rowcum = sb(nc, st, "rowcum", [128, 256])
        pfx = sb(nc, st, "pfx", [128, 1])
        cglob = sb(nc, st, "cglob", [128, 256])
        keep = sb(nc, st, "keep", [128, 256])
        slotp = sb(nc, st, "slotp", [128, 256])
        slot_u = sb(nc, st, "slot_u", [128, 256], U32)
        tv_u = sb(nc, st, "tv_u", [128, 256], U32)
        tv_f = sb(nc, st, "tv_f", [128, 256])
        idx_u = sb(nc, st, "idx_u", [128, 1], U32)
        idxc_f = sb(nc, st, "idxc_f", [128, 1])
        idxc_u = sb(nc, st, "idxc_u", [128, 1], U32)
        gcol = sb(nc, st, "gcol", [128, 1])
        w1_bf = sb(nc, st, "w1_bf", [128, 8 * 4096], BF16)
        w2_bf = sb(nc, st, "w2_bf", [128, 32 * 1024], BF16)
        stg = sb(nc, st, "stg", [128, 4096])
        xg_t = sb(nc, st, "xg_t", [128, D])
        xg_bf = sb(nc, st, "xg_bf", [128, D], BF16)
        xgT_bf = sb(nc, st, "xgT_bf", [128, 8 * 128], BF16)
        h_bf = sb(nc, st, "h_bf", [128, 4096], BF16)
        hT_bf = sb(nc, st, "hT_bf", [128, 32 * 128], BF16)
        out_t = sb(nc, st, "out_t", [128, D])
        bankA = ps(nc, st, "bankA", [128, 512])
        bankT = ps(nc, st, "bankT", [128, 256], BF16)
        bankP = ps(nc, st, "bankP", [128, 512])

        for j in range(8):
            s.dma(stg[:], w1[j * 128:(j + 1) * 128, :])
            s.scopy(w1_bf[:, j * 4096:(j + 1) * 4096], stg[:])
        for j in range(32):
            s.dma(stg[:, 0:1024], w2[j * 128:(j + 1) * 128, :])
            s.scopy(w2_bf[:, j * 1024:(j + 1) * 1024], stg[:, 0:1024])

        s.dma(P_s[:], probsf.rearrange("(p f) e -> p (f e)", p=128))
        P3 = P_s[:].rearrange("p (f e) -> p f e", e=8)
        s.red(m1[:], P3, ALU.max)
        W3 = w2_s[:].rearrange("p (f e) -> p f e", e=8)
        for ee in range(8):
            s.tt(eqt[:], P3[:, :, ee], m1[:], ALU.is_ge)
            s.ts(tmp[:], P3[:, :, ee], 1.0, None, ALU.add)
            s.tt(tmp[:], tmp[:], eqt[:], ALU.mult)
            s.tt(W3[:, :, ee], P3[:, :, ee], tmp[:], ALU.subtract)
        s.red(m2[:], W3, ALU.max)
        s.tt(mask[:], P3[:, :, e], m2[:], ALU.is_ge)
        s.tt(gate[:], P3[:, :, e], mask[:], ALU.mult)
        s.memset(zeros[:], 0.0)
        s.scan_add(rowcum[:], mask[:], zeros[:])
        s.mm(bankP[:, 0:1], su_s[:], rowcum[:, 255:256], start=True, stop=True)
        s.copy(pfx[:], bankP[:, 0:1])
        s.ts(cglob[:], rowcum[:], pfx[:, 0:1], None, ALU.add)
        s.ts(keep[:], cglob[:], float(MOE_CAP), None, ALU.is_le)
        s.tt(keep[:], keep[:], mask[:], ALU.mult)
        s.dma(tv_f[:], tvals_f)
        s.ts(slotp[:], cglob[:], -1.0, None, ALU.add)
        s.ts(tmp[:], tv_f[:], float(MOE_CAP), None, ALU.add)
        s.tt(slotp[:], slotp[:], tmp[:], ALU.subtract)
        s.tt(slotp[:], slotp[:], keep[:], ALU.mult)
        s.tt(slotp[:], slotp[:], tmp[:], ALU.add)
        s.copy(slot_u[:], slotp[:])
        s.dma(tv_u[:, 0:40], prefill)
        s.dma(idxbuf[0:MOE_CAP, 0:1].rearrange("(p f) one -> p (f one)", p=128),
              tv_u[:, 0:40])
        s.dma(gatebuf[0:MOE_CAP, 0:1].rearrange("(p f) one -> p (f one)", p=128),
              zeros[:, 0:40])
        s.dma(tv_u[:], tvals)
        for f in range(256):
            s.idma_scatter(idxbuf, bass.IndirectOffsetOnAxis(slot_u[:, f:f + 1], 0),
                           tv_u[:, f:f + 1])
        for f in range(256):
            s.idma_scatter(gatebuf, bass.IndirectOffsetOnAxis(slot_u[:, f:f + 1], 0),
                           gate[:, f:f + 1])
        for g in range(MOE_CAP // 128):
            s.dma(idx_u[:], idxbuf[g * 128:(g + 1) * 128, :])
            s.copy(idxc_f[:], idx_u[:])
            s.ts(idxc_f[:], idxc_f[:], float(NTOK - 1), None, ALU.min)
            s.copy(idxc_u[:], idxc_f[:])
            s.dma(gcol[:], gatebuf[g * 128:(g + 1) * 128, :])
            s.idma_gather(xg_t[:], xn1f, bass.IndirectOffsetOnAxis(idxc_u[:, 0:1], 0))
            s.scopy(xg_bf[:], xg_t[:])
            for j in range(8):
                s.trans(bankT[:, 0:128], xg_bf[:, j * 128:(j + 1) * 128], ident_bf[:])
                s.copy(xgT_bf[:, j * 128:(j + 1) * 128], bankT[:, 0:128])
            for ns in range(8):
                for kk in range(8):
                    s.mm(bankA[:, 0:512], xgT_bf[:, kk * 128:(kk + 1) * 128],
                         w1_bf[:, kk * 4096 + ns * 512: kk * 4096 + (ns + 1) * 512],
                         start=(kk == 0), stop=(kk == 7))
                s.emit("scalar", lambda e2, o=h_bf, nss=ns, pp=bankA: e2.activation(
                    o[:, nss * 512:(nss + 1) * 512], pp[:, 0:512], AF.Relu), 1)
            for j in range(32):
                s.trans(bankT[:, 0:128], h_bf[:, j * 128:(j + 1) * 128], ident_bf[:])
                s.copy(hT_bf[:, j * 128:(j + 1) * 128], bankT[:, 0:128])
            for half in range(2):
                for kk in range(32):
                    s.mm(bankA[:, 0:512], hT_bf[:, kk * 128:(kk + 1) * 128],
                         w2_bf[:, kk * 1024 + half * 512: kk * 1024 + half * 512 + 512],
                         start=(kk == 0), stop=(kk == 31))
                s.ts(out_t[:, half * 512:(half + 1) * 512], bankA[:, 0:512],
                     gcol[:, 0:1], None, ALU.mult)
            s.idma_scatter(moe_o, bass.IndirectOffsetOnAxis(idx_u[:, 0:1], 0), out_t[:])
        s.finalize(nc)
    print("p5 instrs:", len(s.q), "sem total:", s.cnt)
    return nc


def build_p6():
    nc = mk_nc()
    s = Ser()
    x1 = nc.dram_tensor("x1", [TPC, D], F32, kind="ExternalInput").ap()
    attn1 = nc.dram_tensor("attn1", [TPC, D], F32, kind="ExternalInput").ap()
    moes = [nc.dram_tensor("moe%d" % e, [TPC, D], F32, kind="ExternalInput").ap()
            for e in range(8)]
    out_o = nc.dram_tensor("out", [TPC, D], F32, kind="ExternalOutput").ap()
    with contextlib.ExitStack() as st:
        a = sb(nc, st, "a", [128, D])
        bbuf = sb(nc, st, "b", [128, D])
        for ti in range(TPC // 128):
            r0 = ti * 128
            s.dma(a[:], x1[r0:r0 + 128, :])
            s.dma(bbuf[:], attn1[r0:r0 + 128, :])
            s.tt(a[:], a[:], bbuf[:], ALU.add)
            for e in range(8):
                s.dma(bbuf[:], moes[e][r0:r0 + 128, :])
                s.tt(a[:], a[:], bbuf[:], ALU.add)
            s.dma(out_o[r0:r0 + 128, :], a[:])
        s.finalize(nc)
    print("p6 instrs:", len(s.q), "sem total:", s.cnt)
    return nc


def run_p5_perm(inputs, xn1_full, probs_full):
    consts = make_consts()
    tvals = (np.arange(128, dtype=np.uint32)[:, None] * 256 +
             np.arange(256, dtype=np.uint32)[None, :])
    prefill = (NTOK + np.arange(MOE_CAP, dtype=np.uint32)).reshape(128, 40)
    nc = build_p5(0)
    in_maps = []
    for e in range(NCORES):
        pperm = np.ascontiguousarray(np.roll(probs_full, -e, axis=1))
        in_maps.append({
            "xn1f": xn1_full,
            "probsf": pperm,
            "w1": np.ascontiguousarray(inputs["L1_ffn_experts_inter"][e]),
            "w2": np.ascontiguousarray(inputs["L1_ffn_experts_out"][e]),
            "tvals": tvals,
            "tvals_f": tvals.astype(np.float32),
            "prefill": prefill,
            "c_su_mask": consts["su_mask"],
            "c_ident": consts["ident"],
        })
    res = run_bass_kernel_spmd(nc, in_maps, core_ids=list(range(NCORES)))
    return [r["moe"][:NTOK] for r in res.results]


def run_p6(xfull_x1, attn1_list, moe_list):
    nc = build_p6()
    in_maps = []
    for c in range(NCORES):
        m = {"x1": np.ascontiguousarray(xfull_x1[c * TPC:(c + 1) * TPC]),
             "attn1": attn1_list[c]}
        for e in range(8):
            m["moe%d" % e] = np.ascontiguousarray(moe_list[e][c * TPC:(c + 1) * TPC])
        in_maps.append(m)
    res = run_bass_kernel_spmd(nc, in_maps, core_ids=list(range(NCORES)))
    return [r["out"] for r in res.results]


# version 05 (final)
# speedup vs baseline: 1.0719x; 1.0719x over previous
import math
import contextlib
import numpy as np
import concourse.bass as bass
import concourse.mybir as mybir
from concourse.bass_utils import run_bass_kernel_spmd


def sb(nc, st, name, shape, dtype=None):
    return st.enter_context(nc.sbuf_tensor(name, shape, dtype or F32))


def ps(nc, st, name, shape, dtype=None):
    return st.enter_context(nc.psum_tensor(name, shape, dtype or F32))

F32 = mybir.dt.float32
BF16 = mybir.dt.bfloat16
U32 = mybir.dt.uint32
AF = mybir.ActivationFunctionType
ALU = mybir.AluOpType
AX = mybir.AxisListType

NCORES = 8
B_ALL, SEQ, D = 16, 2048, 1024
BPC = B_ALL // NCORES
TPC = BPC * SEQ
NTOK = B_ALL * SEQ
H0 = 4
DH0 = D // H0
CHUNK = 128
EPS = 1e-5


class Ser:
    def __init__(self):
        self.q = []
        self.cnt = 0

    def emit(self, ename, fn, inc):
        self.q.append((ename, self.cnt, fn, inc))
        self.cnt += inc

    def dma(self, out, in_, ename="sync", **kw):
        self.emit(ename, lambda e: e.dma_start(out=out, in_=in_, **kw), 16)

    def mm(self, out, lhsT, rhs, start=True, stop=True, chain=False):
        if not chain:
            self.emit("tensor", lambda e: e.matmul(out, lhsT, rhs, start=start, stop=stop), 1)
            return
        wait_at = self.cnt if start else None
        inc = 1 if stop else 0
        self.q.append(("tensor", wait_at,
                       (lambda e: e.matmul(out, lhsT, rhs, start=start, stop=stop)), inc))
        self.cnt += inc

    def trans(self, out, in_, ident):
        self.emit("tensor", lambda e: e.transpose(out, in_, ident), 1)

    def act(self, out, in_, func, bias=None, scale=None):
        kw = {}
        if bias is not None:
            kw["bias"] = bias
        if scale is not None:
            kw["scale"] = scale
        self.emit("scalar", lambda e: e.activation(out, in_, func, **kw), 1)

    def copy(self, out, in_):
        self.emit("vector", lambda e: e.tensor_copy(out, in_), 1)

    def scopy(self, out, in_):
        self.emit("scalar", lambda e: e.copy(out, in_), 1)

    def tt(self, out, a, b, op):
        self.emit("vector", lambda e: e.tensor_tensor(out, a, b, op), 1)

    def ts(self, out, in_, s1, s2=None, op0=ALU.add, op1=None):
        if s2 is None:
            self.emit("vector", lambda e: e.tensor_scalar(out, in_, s1, None, op0), 1)
        else:
            self.emit("vector", lambda e: e.tensor_scalar(out, in_, s1, s2, op0, op1), 1)

    def stt(self, out, in0, s, in1, op0, op1):
        self.emit("vector", lambda e: e.scalar_tensor_tensor(out, in0, s, in1, op0, op1), 1)

    def red(self, out, in_, op, absval=False, negate=False):
        self.emit("vector", lambda e: e.tensor_reduce(out, in_, AX.X, op,
                                                      apply_absolute_value=absval,
                                                      negate=negate), 1)

    def ttr(self, accum, a, b, scr):
        self.emit("scalar", lambda e: e.activation(scr, a, AF.Square), 1)
        self.emit("vector", lambda e: e.tensor_reduce(scr if False else accum, scr, AX.X, ALU.add), 1)

    def recip(self, out, in_):
        self.emit("vector", lambda e: e.reciprocal(out, in_), 1)

    def memset(self, ap, v):
        self.emit("vector", lambda e: e.memset(ap, v), 1)

    def scan_add(self, out, data0, zeros, initial=0.0):
        self.emit("vector", lambda e: e.tensor_tensor_scan(out, data0, zeros, initial,
                                                           op0=ALU.add, op1=ALU.add), 1)

    def select(self, out, mask, t, f):
        self.emit("vector", lambda e: e.select(out, mask, t, f), 1)

    def idma_gather(self, out, in_, offs):
        self.emit("gpsimd", lambda e: e.indirect_dma_start(
            out, None, in_, offs), 16)

    def idma_scatter(self, out, offs, in_):
        self.emit("gpsimd", lambda e: e.indirect_dma_start(
            out, offs, in_, None), 16)

    def finalize(self, nc):
        with nc.semaphore("gsem") as sem, nc.Block() as blk:
            q = self.q
            total = self.cnt
            for ename in ("sync", "scalar", "vector", "tensor", "gpsimd"):
                ops = [(at, fn, inc) for (en, at, fn, inc) in q if en == ename]
                if not ops and ename != "gpsimd":
                    continue
                dec = getattr(blk, ename)

                def body(eng, ops=ops, ename=ename):
                    for (at, fn, inc) in ops:
                        if at is not None:
                            eng.wait_ge(sem, at)
                        inst = fn(eng)
                        if inc:
                            inst.then_inc(sem, inc)
                    if ename == "gpsimd":
                        eng.wait_ge(sem, total)
                dec(body)


def mk_nc():
    return bass.Bass("TRN2", target_bir_lowering=False)


def rne_round_abs(s, out, a, frac_scr):
    s.ts(frac_scr, a, 0.5, 1.0, ALU.add, ALU.mod)
    s.stt(out, a, 0.5, frac_scr, ALU.add, ALU.subtract)



def make_consts():
    c = {}
    c["ident"] = np.eye(128, dtype=np.float32)
    c["ones_col"] = np.ones((128, 1), dtype=np.float32)
    c["ones_row"] = np.ones((1, 128), dtype=np.float32)
    su = np.triu(np.ones((128, 128), dtype=np.float32), 1)
    c["su_mask"] = su
    c["sl_mask"] = su.T.copy()
    c["ui_mask"] = np.triu(np.ones((128, 128), dtype=np.float32), 0)
    cm = np.zeros((128, 128), dtype=np.float32)
    cm[np.triu_indices(128, 1)] = -1e30
    c["causal_add"] = cm
    cc = np.zeros((128, 5), dtype=np.float32)
    cc[:, 0] = 1e-5; cc[:, 1] = 1.0; cc[:, 2] = 0.5; cc[:, 3] = 1e-4; cc[:, 4] = 128.5
    c["ccols"] = cc
    return c


def decl_consts(nc, s, c, need=("ident",)):
    tiles = {}
    handles = {}
    for name in need:
        arr = c[name]
        d = nc.dram_tensor("c_" + name, list(arr.shape), F32, kind="ExternalInput")
        t = nc.sbuf_tensor("cs_" + name, list(arr.shape), F32)
        t.__enter__()
        s.dma(t[:], d.ap())
        tiles[name] = t
        handles[name] = arr
    return tiles, handles



def build_p1():
    nc = mk_nc()
    s = Ser()
    x = nc.dram_tensor("x", [TPC, D], F32, kind="ExternalInput").ap()
    wqT = nc.dram_tensor("wqT", [D, D], F32, kind="ExternalInput").ap()
    wkT = nc.dram_tensor("wkT", [D, D], F32, kind="ExternalInput").ap()
    wvT = nc.dram_tensor("wvT", [D, D], F32, kind="ExternalInput").ap()
    wbT = nc.dram_tensor("wbT", [D, H0], F32, kind="ExternalInput").ap()
    cw = {}
    for nm in ("qcw", "kcw", "vcw"):
        cw[nm] = nc.dram_tensor(nm, [D, 4], F32, kind="ExternalInput").ap()

    xn_o = nc.dram_tensor("xn", [TPC, D], F32, kind="ExternalOutput").ap()
    xnT_o = nc.dram_tensor("xnT", [D, TPC], F32, kind="ExternalOutput").ap()
    qc_o = nc.dram_tensor("qc", [D, TPC], F32, kind="ExternalOutput").ap()
    kc_o = nc.dram_tensor("kc", [D, TPC], F32, kind="ExternalOutput").ap()
    vc_o = nc.dram_tensor("vc", [D, TPC], F32, kind="ExternalOutput").ap()
    beta_o = nc.dram_tensor("beta", [H0, TPC], F32, kind="ExternalOutput").ap()

    with contextlib.ExitStack() as st:
        ident = sb(nc, st, "ident", [128, 128])
        xt = sb(nc, st, "xt", [128, D])
        xnt = sb(nc, st, "xnt", [128, D])
        sq_scr = sb(nc, st, "sq_scr", [128, D])
        st1 = sb(nc, st, "st1", [128, 1])
        st2 = sb(nc, st, "st2", [128, 1])
        st3 = sb(nc, st, "st3", [128, 1])
        wq_s = sb(nc, st, "wq_s", [128, D])
        wb_s = sb(nc, st, "wb_s", [128, H0 * 8])
        xnT_blk = sb(nc, st, "xnT_blk", [128, 8 * 512])
        cwq_t = sb(nc, st, "cwq", [128, 8 * 4])
        cwk_t = sb(nc, st, "cwk", [128, 8 * 4])
        cwv_t = sb(nc, st, "cwv", [128, 8 * 4])
        ext = sb(nc, st, "ext", [128, 3 + 512])
        acc = sb(nc, st, "acc", [128, 512])
        tails = sb(nc, st, "tails", [128, 3 * 24])
        outb = sb(nc, st, "outb", [128, 512])
        psA = ps(nc, st, "psA", [128, 512])
        psB = ps(nc, st, "psB", [128, 512])
        psT = ps(nc, st, "psT", [128, 128])
        cdram = nc.dram_tensor("c_ident", [128, 128], F32, kind="ExternalInput")
        s.dma(ident[:], cdram.ap())
        ccols = sb(nc, st, "ccols", [128, 5])
        s.dma(ccols[:], nc.dram_tensor("c_ccols", [128, 5], F32, kind="ExternalInput").ap())
        epsb = ccols[:, 0:1]
        for j in range(8):
            s.dma(cwq_t[:, j * 4:(j + 1) * 4], cw["qcw"][j * 128:(j + 1) * 128, :])
            s.dma(cwk_t[:, j * 4:(j + 1) * 4], cw["kcw"][j * 128:(j + 1) * 128, :])
            s.dma(cwv_t[:, j * 4:(j + 1) * 4], cw["vcw"][j * 128:(j + 1) * 128, :])

        nblk = TPC // 512
        for blk in range(nblk):
            t0 = blk * 512
            batch_start = (t0 % SEQ) == 0
            for sub in range(4):
                r0 = t0 + sub * 128
                s.dma(xt[:], x[r0:r0 + 128, :])
                s.red(st1[:], xt[:], ALU.add)
                s.ttr(st2[:], xt[:], xt[:], sq_scr[:])
                s.ts(st1[:], st1[:], 1.0 / D, None, ALU.mult)
                s.ts(st2[:], st2[:], 1.0 / D, None, ALU.mult)
                s.emit("scalar", lambda e, o=st3, i=st1: e.square(o[:], i[:]), 1)
                s.tt(st2[:], st2[:], st3[:], ALU.subtract)
                s.act(st3[:], st2[:], AF.Sqrt, bias=epsb)
                s.recip(st3[:], st3[:])
                s.emit("vector", lambda e, o=xnt, i=xt, m=st1, r=st3: e.tensor_scalar(
                    o[:], i[:], m[:], r[:], ALU.subtract, ALU.mult), 1)
                s.dma(xn_o[r0:r0 + 128, :], xnt[:])
                for j in range(8):
                    s.trans(psT[:], xnt[:, j * 128:(j + 1) * 128], ident[:])
                    s.copy(xnT_blk[:, j * 512 + sub * 128: j * 512 + sub * 128 + 128], psT[:])
            for j in range(8):
                s.dma(xnT_o[j * 128:(j + 1) * 128, t0:t0 + 512], xnT_blk[:, j * 512:(j + 1) * 512])

            for (wT, cwt, out_h, psm) in ((wqT, cwq_t, qc_o, psA),
                                          (wkT, cwk_t, kc_o, psB),
                                          (wvT, cwv_t, vc_o, psA)):
                pofs = (0 if out_h is qc_o else (1 if out_h is kc_o else 2)) * 8
                for j in range(8):
                    for kk in range(8):
                        s.dma(wq_s[:, 0:128], wT[kk * 128:(kk + 1) * 128, j * 128:(j + 1) * 128])
                        s.mm(psm[:], wq_s[:, 0:128], xnT_blk[:, kk * 512:(kk + 1) * 512],
                             start=(kk == 0), stop=(kk == 7))
                    tcol = (pofs + j) * 3
                    if batch_start:
                        s.memset(ext[:, 0:3], 0.0)
                    else:
                        s.copy(ext[:, 0:3], tails[:, tcol:tcol + 3])
                    s.copy(ext[:, 3:515], psm[:])
                    s.copy(tails[:, tcol:tcol + 3], ext[:, 512:515])
                    w4 = cwt[:, j * 4 + 3: j * 4 + 4]
                    s.ts(acc[:], ext[:, 3:515], w4, None, ALU.mult)
                    for tap in range(3):
                        wt = cwt[:, j * 4 + tap: j * 4 + tap + 1]
                        s.stt(acc[:], ext[:, tap:tap + 512], wt, acc[:], ALU.mult, ALU.add)
                    s.act(outb[:], acc[:], AF.Silu)
                    s.dma(out_h[j * 128:(j + 1) * 128, t0:t0 + 512], outb[:])

            for kk in range(8):
                s.dma(wb_s[:, kk * H0:(kk + 1) * H0], wbT[kk * 128:(kk + 1) * 128, :])
            for kk in range(8):
                s.mm(psB[0:H0, 0:512], wb_s[:, kk * H0:(kk + 1) * H0],
                     xnT_blk[:, kk * 512:(kk + 1) * 512], start=(kk == 0), stop=(kk == 7))
            s.act(outb[0:H0, 0:512], psB[0:H0, 0:512], AF.Sigmoid)
            s.dma(beta_o[:, t0:t0 + 512], outb[0:H0, 0:512])

        s.finalize(nc)
    return nc


def run_p1(inputs, xfull):
    nc = build_p1()
    consts = make_consts()
    in_maps = []
    for c in range(NCORES):
        m = {
            "x": np.ascontiguousarray(xfull[c * TPC:(c + 1) * TPC]),
            "wqT": np.ascontiguousarray(inputs["L0_attn_q_proj_w"].T),
            "wkT": np.ascontiguousarray(inputs["L0_attn_k_proj_w"].T),
            "wvT": np.ascontiguousarray(inputs["L0_attn_v_proj_w"].T),
            "wbT": np.ascontiguousarray(inputs["L0_attn_b_proj_w"].T),
            "qcw": inputs["L0_attn_q_conv_w"],
            "kcw": inputs["L0_attn_k_conv_w"],
            "vcw": inputs["L0_attn_v_conv_w"],
            "c_ident": consts["ident"],
            "c_ccols": consts["ccols"],
        }
        in_maps.append(m)
    res = run_bass_kernel_spmd(nc, in_maps, core_ids=list(range(NCORES)))
    return res.results



def kernel(**inputs):
    xfull = np.ascontiguousarray(
        np.asarray(inputs["L0_attn_hidden_states"], dtype=np.float32).reshape(NTOK, D))
    r1 = run_p1(inputs, xfull)
    delta = run_p2(r1)
    attn0 = run_p3(inputs, r1, delta)
    r3b = run_p3b(inputs, xfull, r1, attn0)
    attn1 = run_p4(inputs, r3b)
    xn1_full = np.ascontiguousarray(np.concatenate([r["xn1"] for r in r3b], 0))
    probs_full = np.ascontiguousarray(np.concatenate([r["probs"] for r in r3b], 0))
    x1_full = np.concatenate([r["x1"] for r in r3b], 0)
    moe = run_p5_perm(inputs, xn1_full, probs_full)
    out = run_p6(x1_full, attn1, moe)
    return np.concatenate(out, 0).reshape(B_ALL, SEQ, D)


def build_p2(seq_range):
    nc = mk_nc()
    s = Ser()
    qc = nc.dram_tensor("qc", [D, TPC], F32, kind="ExternalInput").ap()
    kc = nc.dram_tensor("kc", [D, TPC], F32, kind="ExternalInput").ap()
    vc = nc.dram_tensor("vc", [D, TPC], F32, kind="ExternalInput").ap()
    beta = nc.dram_tensor("beta", [H0, TPC], F32, kind="ExternalInput").ap()
    delta_o = nc.dram_tensor("delta", [TPC, D], F32, kind="ExternalOutput").ap()

    with contextlib.ExitStack() as st:
        ident = sb(nc, st, "ident", [128, 128])
        ones_row = sb(nc, st, "ones_row", [1, 128])
        ones_col = sb(nc, st, "ones_col", [128, 1])
        su_m = sb(nc, st, "su_m", [128, 128])
        ui_m = sb(nc, st, "ui_m", [128, 128])
        s.dma(ident[:], nc.dram_tensor("c_ident", [128, 128], F32, kind="ExternalInput").ap())
        s.dma(ones_row[:], nc.dram_tensor("c_ones_row", [1, 128], F32, kind="ExternalInput").ap())
        s.dma(ones_col[:], nc.dram_tensor("c_ones_col", [128, 1], F32, kind="ExternalInput").ap())
        s.dma(su_m[:], nc.dram_tensor("c_su_mask", [128, 128], F32, kind="ExternalInput").ap())
        s.dma(ui_m[:], nc.dram_tensor("c_ui_mask", [128, 128], F32, kind="ExternalInput").ap())

        qT2 = sb(nc, st, "qT2", [128, 2 * 2048])
        kT2 = sb(nc, st, "kT2", [128, 2 * 2048])
        vT2 = sb(nc, st, "vT2", [128, 2 * 2048])
        beta_16 = sb(nc, st, "beta_16", [16, 128])
        beta_cm = sb(nc, st, "beta_cm", [128, 16])
        beta_row = sb(nc, st, "beta_row", [1, 2048])
        rq = sb(nc, st, "rq", [1, 2048])
        rk = sb(nc, st, "rk", [1, 2048])
        rkb = sb(nc, st, "rkb", [1, 2048])
        sq_scr = sb(nc, st, "sq_scr", [128, 512])
        S0 = sb(nc, st, "S0", [128, 256])
        S1 = sb(nc, st, "S1", [128, 256])
        qn_s = sb(nc, st, "qn_s", [128, 256])
        kn_s = sb(nc, st, "kn_s", [128, 256])
        kb_s = sb(nc, st, "kb_s", [128, 256])
        kn_tm = sb(nc, st, "kn_tm", [128, 256])
        kb_tm = sb(nc, st, "kb_tm", [128, 256])
        v_tm = sb(nc, st, "v_tm", [128, 256])
        vb_tm = sb(nc, st, "vb_tm", [128, 256])
        U_s = sb(nc, st, "U_s", [128, 128])
        L_s = sb(nc, st, "L_s", [128, 128])
        Wa = sb(nc, st, "Wa", [128, 128])
        Wb = sb(nc, st, "Wb", [128, 128])
        Va = sb(nc, st, "Va", [128, 128])
        Vb = sb(nc, st, "Vb", [128, 128])
        Pa = sb(nc, st, "Pa", [128, 128])
        Pb = sb(nc, st, "Pb", [128, 128])
        u_s = sb(nc, st, "u_s", [128, 256])
        u_adj = sb(nc, st, "u_adj", [128, 256])
        wT_s = sb(nc, st, "wT_s", [128, 256])
        at_s = sb(nc, st, "at_s", [128, 128])
        o_s = sb(nc, st, "o_s", [128, 256])

        bankA = ps(nc, st, "bankA", [128, 512])
        bankB = ps(nc, st, "bankB", [128, 512])
        bankC = ps(nc, st, "bankC", [128, 512])
        bankD = ps(nc, st, "bankD", [128, 512])

        R_bc = bankA[:, 0:128]
        R_mm = bankA[:, 128:256]
        R_tr = bankA[:, 256:384]
        psO = bankB[:, 0:256]
        psO2 = bankB[:, 256:512]
        psO3 = bankC[:, 0:256]
        psS = bankC[:, 256:512]
        ps1 = bankD[0:1, 0:512]

        for (b, h) in seq_range:
            c0 = b * 2048
            r0 = h * 256
            for (dst, src) in ((qT2, qc), (kT2, kc), (vT2, vc)):
                s.dma(dst[:, 0:2048], src[r0:r0 + 128, c0:c0 + 2048])
                s.dma(dst[:, 2048:4096], src[r0 + 128:r0 + 256, c0:c0 + 2048])
            s.dma(beta_16[:], beta[h:h + 1, c0:c0 + 2048].rearrange("o (f p) -> (o f) p", p=128))
            s.trans(R_tr[0:128, 0:16], beta_16[:], ident[0:16, 0:16])
            s.copy(beta_cm[:], R_tr[0:128, 0:16])
            s.dma(beta_row[:], beta[h:h + 1, c0:c0 + 2048])
            for (src, dst) in ((qT2, rq), (kT2, rk)):
                for j4 in range(4):
                    cc = j4 * 512
                    s.act(sq_scr[:], src[:, cc:cc + 512], AF.Square)
                    s.mm(ps1[:, 0:512], ones_col[:], sq_scr[:], start=True, stop=False)
                    s.act(sq_scr[:], src[:, 2048 + cc:2048 + cc + 512], AF.Square)
                    s.mm(ps1[:, 0:512], ones_col[:], sq_scr[:], start=False, stop=True)
                    s.act(dst[:, cc:cc + 512], ps1[:, 0:512], AF.Sqrt)
                    s.recip(dst[:, cc:cc + 512], dst[:, cc:cc + 512])
            s.tt(rkb[:], rk[:], beta_row[:], ALU.mult)
            s.memset(S0[:], 0.0)
            s.memset(S1[:], 0.0)

            for ci in range(16):
                cc = ci * 128
                s.mm(R_bc, ones_row[:], rq[0:1, cc:cc + 128])
                s.tt(qn_s[:, 0:128], qT2[:, cc:cc + 128], R_bc, ALU.mult)
                s.tt(qn_s[:, 128:256], qT2[:, 2048 + cc:2048 + cc + 128], R_bc, ALU.mult)
                s.mm(R_bc, ones_row[:], rk[0:1, cc:cc + 128])
                s.tt(kn_s[:, 0:128], kT2[:, cc:cc + 128], R_bc, ALU.mult)
                s.tt(kn_s[:, 128:256], kT2[:, 2048 + cc:2048 + cc + 128], R_bc, ALU.mult)
                s.mm(R_bc, ones_row[:], rkb[0:1, cc:cc + 128])
                s.tt(kb_s[:, 0:128], kT2[:, cc:cc + 128], R_bc, ALU.mult)
                s.tt(kb_s[:, 128:256], kT2[:, 2048 + cc:2048 + cc + 128], R_bc, ALU.mult)
                for half in range(2):
                    s.trans(R_tr, kn_s[:, half * 128:half * 128 + 128], ident[:])
                    s.copy(kn_tm[:, half * 128:half * 128 + 128], R_tr)
                    s.trans(R_tr, vT2[:, half * 2048 + cc:half * 2048 + cc + 128], ident[:])
                    s.copy(v_tm[:, half * 128:half * 128 + 128], R_tr)
                s.ts(kb_tm[:], kn_tm[:], beta_cm[:, ci:ci + 1], None, ALU.mult)
                s.ts(vb_tm[:], v_tm[:], beta_cm[:, ci:ci + 1], None, ALU.mult)
                s.mm(R_mm, kn_s[:, 0:128], kb_s[:, 0:128], start=True, stop=False)
                s.mm(R_mm, kn_s[:, 128:256], kb_s[:, 128:256], start=False, stop=True)
                s.tt(U_s[:], R_mm, su_m[:], ALU.mult)
                s.trans(R_tr, U_s[:], ident[:])
                s.copy(L_s[:], R_tr)
                s.tt(Pa[:], ident[:], U_s[:], ALU.subtract)
                Wc, Vc = U_s, L_s
                Pcur, Pnxt = Pa, Pb
                Wn, Vn = Wa, Va
                for lev in range(1, 7):
                    s.mm(R_mm, Vc[:], Wc[:], start=True, stop=True)
                    s.copy(Wn[:], R_mm)
                    s.mm(R_mm, Wc[:], Vc[:], start=True, stop=True)
                    s.copy(Vn[:], R_mm)
                    s.mm(R_mm, Vn[:], Pcur[:], start=True, stop=False)
                    s.mm(R_mm, ident[:], Pcur[:], start=False, stop=True)
                    s.copy(Pnxt[:], R_mm)
                    Wc, Vc = Wn, Vn
                    Wn, Vn = (Wb, Vb) if Wn is Wa else (Wa, Va)
                    Pcur, Pnxt = Pnxt, Pcur
                TT = Pcur
                s.mm(psO, TT[:], vb_tm[:], start=True, stop=True)
                s.copy(u_s[:], psO)
                s.mm(R_mm, kb_tm[:, 0:128], TT[:], start=True, stop=True)
                s.copy(wT_s[:, 0:128], R_mm)
                s.mm(R_mm, kb_tm[:, 128:256], TT[:], start=True, stop=True)
                s.copy(wT_s[:, 128:256], R_mm)
                s.mm(psO2, wT_s[:, 0:128], S0[:], start=True, stop=False)
                s.mm(psO2, wT_s[:, 128:256], S1[:], start=False, stop=True)
                s.tt(u_adj[:], u_s[:], psO2, ALU.subtract)
                s.mm(R_mm, kn_s[:, 0:128], qn_s[:, 0:128], start=True, stop=False)
                s.mm(R_mm, kn_s[:, 128:256], qn_s[:, 128:256], start=False, stop=True)
                s.tt(at_s[:], R_mm, ui_m[:], ALU.mult)
                s.mm(psO3, qn_s[:, 0:128], S0[:], start=True, stop=False)
                s.mm(psO3, qn_s[:, 128:256], S1[:], start=False, stop=False)
                s.mm(psO3, at_s[:], u_adj[:], start=False, stop=True)
                s.copy(o_s[:], psO3)
                s.dma(delta_o[b * 2048 + cc: b * 2048 + cc + 128, r0:r0 + 256], o_s[:])
                s.mm(psS, kn_tm[:, 0:128], u_adj[:], start=True, stop=True)
                s.tt(S0[:], S0[:], psS, ALU.add)
                s.mm(psS, kn_tm[:, 128:256], u_adj[:], start=True, stop=True)
                s.tt(S1[:], S1[:], psS, ALU.add)
        s.finalize(nc)
    print("p2 instrs:", len(s.q), "sem total:", s.cnt)
    return nc


def run_p2(r1_results):
    consts = make_consts()
    seqs = [(b, h) for b in range(BPC) for h in range(H0)]
    outs = []
    for part in (seqs[:4], seqs[4:]):
        nc = build_p2(part)
        in_maps = []
        for c in range(NCORES):
            m = {
                "qc": r1_results[c]["qc"],
                "kc": r1_results[c]["kc"],
                "vc": r1_results[c]["vc"],
                "beta": r1_results[c]["beta"],
                "c_ident": consts["ident"],
                "c_ones_row": consts["ones_row"],
                "c_ones_col": consts["ones_col"],
                "c_su_mask": consts["su_mask"],
                "c_ui_mask": consts["ui_mask"],
            }
            in_maps.append(m)
        res = run_bass_kernel_spmd(nc, in_maps, core_ids=list(range(NCORES)))
        outs.append(res.results)
    delta = []
    for c in range(NCORES):
        d = np.zeros((TPC, D), np.float32)
        for (part, rr) in zip((seqs[:4], seqs[4:]), outs):
            for (b, h) in part:
                d[b * 2048:(b + 1) * 2048, h * 256:(h + 1) * 256] = \
                    rr[c]["delta"][b * 2048:(b + 1) * 2048, h * 256:(h + 1) * 256]
        delta.append(d)
    return delta


def build_p3(b):
    nc = mk_nc()
    s = Ser()
    vc = nc.dram_tensor("vc", [D, TPC], F32, kind="ExternalInput").ap()
    delta = nc.dram_tensor("delta", [TPC, D], F32, kind="ExternalInput").ap()
    xnT = nc.dram_tensor("xnT", [D, TPC], F32, kind="ExternalInput").ap()
    firs = nc.dram_tensor("firs", [128, 5 * 8], F32, kind="ExternalInput").ap()
    firl = nc.dram_tensor("firl", [128, 64 * 8], F32, kind="ExternalInput").ap()
    w1aT = nc.dram_tensor("w1aT", [D, D], F32, kind="ExternalInput").ap()
    w1bT = nc.dram_tensor("w1bT", [16, D], F32, kind="ExternalInput").ap()
    b1c = nc.dram_tensor("b1c", [128, 8], F32, kind="ExternalInput").ap()
    w2T = nc.dram_tensor("w2T", [D, 4], F32, kind="ExternalInput").ap()
    b2c = nc.dram_tensor("b2c", [4, 1], F32, kind="ExternalInput").ap()
    ltemp = nc.dram_tensor("ltemp", [1, 1], F32, kind="ExternalInput").ap()
    rss = nc.dram_tensor("rss", [1, 1], F32, kind="ExternalInput").ap()
    rsl = nc.dram_tensor("rsl", [1, 1], F32, kind="ExternalInput").ap()
    opT = nc.dram_tensor("opT", [D, D], F32, kind="ExternalInput").ap()
    onwc = nc.dram_tensor("onwc", [128, 8], F32, kind="ExternalInput").ap()
    attn_o = nc.dram_tensor("attn0", [TPC, D], F32, kind="ExternalOutput").ap()

    with contextlib.ExitStack() as st:
        ident = sb(nc, st, "ident", [128, 128])
        ones_row = sb(nc, st, "ones_row", [1, 128])
        ccols = sb(nc, st, "ccols", [128, 5])
        s.dma(ident[:], nc.dram_tensor("c_ident", [128, 128], F32, kind="ExternalInput").ap())
        s.dma(ones_row[:], nc.dram_tensor("c_ones_row", [1, 128], F32, kind="ExternalInput").ap())
        s.dma(ccols[:], nc.dram_tensor("c_ccols", [128, 5], F32, kind="ExternalInput").ap())
        epsb = ccols[:, 0:1]
        oneb = ccols[:, 1:2]

        vcb = sb(nc, st, "vcb", [128, 8 * 576])
        shf = sb(nc, st, "shf", [128, 8 * 512])
        lof = sb(nc, st, "lof", [128, 8 * 512])
        fir_s = sb(nc, st, "fir_s", [128, 40])
        fir_l = sb(nc, st, "fir_l", [128, 512])
        accf = sb(nc, st, "accf", [128, 512])
        w1a_bf = sb(nc, st, "w1a_bf", [128, 8 * 1024], BF16)
        w2_bf = sb(nc, st, "w2_bf", [128, 4 * 8], BF16)
        w1b_s = sb(nc, st, "w1b_s", [16, D])
        b1_s = sb(nc, st, "b1_s", [128, 8])
        b2_s = sb(nc, st, "b2_s", [4, 1])
        stg = sb(nc, st, "stg", [128, 1024])
        opT_s = sb(nc, st, "opT_s", [128, 8 * 1024])
        onw_s = sb(nc, st, "onw_s", [128, 8])
        xnTf = sb(nc, st, "xnTf", [128, 8 * 512])
        xnTb = sb(nc, st, "xnTb", [128, 8 * 512], BF16)
        H1s = sb(nc, st, "H1s", [128, 8 * 512])
        hdn_bf = sb(nc, st, "hdn_bf", [128, 512], BF16)
        statsT = sb(nc, st, "statsT", [16, 4 * 512])
        stats_all = sb(nc, st, "stats_all", [128, 64])
        sq_scr = sb(nc, st, "sq_scr", [128, 1024])
        sc1 = sb(nc, st, "sc1", [128, 4])
        sc2 = sb(nc, st, "sc2", [128, 4])
        sc3 = sb(nc, st, "sc3", [128, 4])
        vd_tm = sb(nc, st, "vd_tm", [128, 1024])
        sh_tm = sb(nc, st, "sh_tm", [128, 1024])
        lo_tm = sb(nc, st, "lo_tm", [128, 1024])
        dl_tm = sb(nc, st, "dl_tm", [128, 1024])
        lg_s = sb(nc, st, "lg_s", [4, 512])
        fw_s = sb(nc, st, "fw_s", [128, 4 * 4])
        o_tm = sb(nc, st, "o_tm", [128, 1024])
        oT_s = sb(nc, st, "oT_s", [128, 8 * 512])
        sp1 = sb(nc, st, "sp1", [1, 1])
        sp2 = sb(nc, st, "sp2", [1, 1])
        sp3 = sb(nc, st, "sp3", [1, 1])
        rtemp_b = sb(nc, st, "rtemp_b", [128, 1])
        rss_b = sb(nc, st, "rss_b", [128, 1])
        rsl_b = sb(nc, st, "rsl_b", [128, 1])

        bank1 = ps(nc, st, "bank1", [128, 512])
        bank2 = ps(nc, st, "bank2", [128, 512])
        bank3 = ps(nc, st, "bank3", [128, 512], BF16)
        bank4 = ps(nc, st, "bank4", [128, 512])
        R_tr = bank1[:, 0:128]
        R_l4 = bank4[0:4, 0:512]

        s.dma(fir_s[:], firs)
        for jj in range(4):
            s.dma(fir_l[:, jj * 128:(jj + 1) * 128], firl[:, jj * 128:(jj + 1) * 128])
        s.dma(w1b_s[:], w1bT)
        s.dma(b1_s[:], b1c)
        s.dma(b2_s[:], b2c)
        for j in range(8):
            s.dma(stg[:], w1aT[j * 128:(j + 1) * 128, :])
            s.scopy(w1a_bf[:, j * 1024:(j + 1) * 1024], stg[:])
            s.dma(stg[:, 0:4], w2T[j * 128:(j + 1) * 128, :])
            s.scopy(w2_bf[:, j * 4:(j + 1) * 4], stg[:, 0:4])
            s.dma(opT_s[:, j * 1024:(j + 1) * 1024], opT[j * 128:(j + 1) * 128, :])
        s.dma(onw_s[:], onwc)
        for j in range(8):
            s.ts(opT_s[:, j * 1024:(j + 1) * 1024], opT_s[:, j * 1024:(j + 1) * 1024],
                 onw_s[:, j:j + 1], None, ALU.mult)
        s.dma(sp1[:], ltemp)
        s.act(sp2[:], sp1[:], AF.Abs)
        s.act(sp2[:], sp2[:], AF.Exp, scale=-1.0)
        s.act(sp2[:], sp2[:], AF.Ln, bias=oneb[0:1, 0:1])
        s.ts(sp3[:], sp1[:], 0.0, None, ALU.max)
        s.tt(sp3[:], sp3[:], sp2[:], ALU.add)
        s.ts(sp3[:], sp3[:], 1e-4, None, ALU.add)
        s.recip(sp3[:], sp3[:])
        s.mm(bank1[:, 0:1], ones_row[:], sp3[:], start=True, stop=True)
        s.copy(rtemp_b[:], bank1[:, 0:1])
        s.dma(sp1[:], rss)
        s.mm(bank1[:, 0:1], ones_row[:], sp1[:], start=True, stop=True)
        s.copy(rss_b[:], bank1[:, 0:1])
        s.dma(sp1[:], rsl)
        s.mm(bank1[:, 0:1], ones_row[:], sp1[:], start=True, stop=True)
        s.copy(rsl_b[:], bank1[:, 0:1])

        for blk in range(4):
            bc0 = blk * 512
            c0 = b * 2048 + bc0
            t0 = c0
            for j in range(8):
                base = j * 576
                if bc0 == 0:
                    s.memset(vcb[:, base:base + 63], 0.0)
                    s.dma(vcb[:, base + 63:base + 575], vc[j * 128:(j + 1) * 128, c0:c0 + 512])
                else:
                    s.dma(vcb[:, base:base + 575], vc[j * 128:(j + 1) * 128, c0 - 63:c0 + 512])
            for j in range(8):
                base = j * 576
                dst_s = shf[:, j * 512:(j + 1) * 512]
                dst_l = lof[:, j * 512:(j + 1) * 512]
                s.ts(accf[:], vcb[:, base + 59:base + 571], fir_s[:, j * 5 + 0: j * 5 + 1], None, ALU.mult)
                for jj in range(1, 5):
                    s.stt(accf[:], vcb[:, base + 59 + jj:base + 571 + jj],
                          fir_s[:, j * 5 + jj: j * 5 + jj + 1], accf[:], ALU.mult, ALU.add)
                s.copy(dst_s, accf[:])
                s.ts(accf[:], vcb[:, base:base + 512], fir_l[:, j * 64:j * 64 + 1], None, ALU.mult)
                for jj in range(1, 64):
                    s.stt(accf[:], vcb[:, base + jj:base + jj + 512],
                          fir_l[:, j * 64 + jj:j * 64 + jj + 1], accf[:], ALU.mult, ALU.add)
                s.copy(dst_l, accf[:])
            for j in range(8):
                s.dma(xnTf[:, j * 512:(j + 1) * 512], xnT[j * 128:(j + 1) * 128, c0:c0 + 512])
                s.scopy(xnTb[:, j * 512:(j + 1) * 512], xnTf[:, j * 512:(j + 1) * 512])
            for j in range(8):
                for kk in range(8):
                    s.mm(bank2[:, 0:512], w1a_bf[:, kk * 1024 + j * 128: kk * 1024 + j * 128 + 128],
                         xnTb[:, kk * 512:(kk + 1) * 512], start=(kk == 0), stop=(kk == 7))
                s.copy(H1s[:, j * 512:(j + 1) * 512], bank2[:, 0:512])

            for sub in range(4):
                cc = bc0 + sub * 128
                tok0 = t0 + sub * 128
                for j in range(8):
                    s.trans(R_tr, vcb[:, j * 576 + 63 + sub * 128: j * 576 + 63 + sub * 128 + 128], ident[:])
                    s.copy(vd_tm[:, j * 128:(j + 1) * 128], R_tr)
                    s.trans(R_tr, shf[:, j * 512 + sub * 128: j * 512 + sub * 128 + 128], ident[:])
                    s.copy(sh_tm[:, j * 128:(j + 1) * 128], R_tr)
                    s.trans(R_tr, lof[:, j * 512 + sub * 128: j * 512 + sub * 128 + 128], ident[:])
                    s.copy(lo_tm[:, j * 128:(j + 1) * 128], R_tr)
                s.dma(dl_tm[:], delta[tok0:tok0 + 128, :])
                for (ix, X) in enumerate((sh_tm, lo_tm, dl_tm, vd_tm)):
                    X3 = X[:].rearrange("p (h d) -> p h d", h=4)
                    s.red(sc1[:], X3, ALU.add)
                    s.act(sq_scr[:], X[:], AF.Square)
                    s.red(sc2[:], sq_scr[:].rearrange("p (h d) -> p h d", h=4), ALU.add)
                    s.ts(stats_all[:].rearrange("p (h ss) -> p h ss", h=4)[:, :, ix * 4 + 0],
                         sc1[:], 1.0 / DH0, None, ALU.mult)
                    s.ts(sc1[:], sc1[:], 1.0 / DH0, None, ALU.mult)
                    s.act(sc3[:], sc1[:], AF.Square)
                    s.ts(sc1[:], sc2[:], 1.0 / DH0, None, ALU.mult)
                    s.tt(stats_all[:].rearrange("p (h ss) -> p h ss", h=4)[:, :, ix * 4 + 1],
                         sc1[:], sc3[:], ALU.subtract)
                    s.red(sc3[:], X3, ALU.add, absval=True)
                    s.ts(stats_all[:].rearrange("p (h ss) -> p h ss", h=4)[:, :, ix * 4 + 2],
                         sc3[:], 1.0 / DH0, None, ALU.mult)
                    s.act(stats_all[:].rearrange("p (h ss) -> p h ss", h=4)[:, :, ix * 4 + 3],
                          sc2[:], AF.Sqrt)
                for h in range(H0):
                    s.trans(bank1[0:16, 128:256], stats_all[:, h * 16:(h + 1) * 16], ident[:])
                    s.copy(statsT[:, h * 512 + sub * 128: h * 512 + sub * 128 + 128], bank1[0:16, 128:256])

                for h in range(H0):
                    for j in range(8):
                        s.mm(bank2[:, 0:128], w1b_s[:, j * 128:(j + 1) * 128],
                             statsT[:, h * 512 + sub * 128: h * 512 + sub * 128 + 128],
                             start=True, stop=True)
                        s.tt(sq_scr[:, 0:128], bank2[:, 0:128],
                             H1s[:, j * 512 + sub * 128: j * 512 + sub * 128 + 128], ALU.add)
                        s.emit("scalar", lambda e, o=hdn_bf, i=sq_scr, bcol=b1_s[:, j:j + 1]:
                               e.activation(o[:, 0:128], i[:, 0:128], AF.Gelu, bias=bcol), 1)
                        s.mm(R_l4[:, 0:128], w2_bf[:, j * 4:(j + 1) * 4], hdn_bf[:, 0:128],
                             start=(j == 0), stop=(j == 7))
                    s.ts(lg_s[:, 0:128], R_l4[:, 0:128], b2_s[:, 0:1], rtemp_b[0:4, 0:1],
                         ALU.add, ALU.mult)
                    s.trans(bank1[0:128, 128:132], lg_s[0:4, 0:128], ident[0:4, 0:4])
                    s.copy(fw_s[:, h * 4:(h + 1) * 4], bank1[0:128, 128:132])
                    s.red(sc1[:, 0:1], fw_s[:, h * 4:(h + 1) * 4], ALU.max, negate=True)
                    s.emit("scalar", lambda e, o=fw_s, hh=h, bcol=sc1[:, 0:1]:
                           e.activation(o[:, hh * 4:(hh + 1) * 4], o[:, hh * 4:(hh + 1) * 4],
                                        AF.Exp, bias=bcol), 1)
                    s.red(sc1[:, 0:1], fw_s[:, h * 4:(h + 1) * 4], ALU.add)
                    s.recip(sc1[:, 0:1], sc1[:, 0:1])
                    s.ts(fw_s[:, h * 4:(h + 1) * 4], fw_s[:, h * 4:(h + 1) * 4],
                         sc1[:, 0:1], None, ALU.mult)
                    s.tt(fw_s[:, h * 4 + 0:h * 4 + 1], fw_s[:, h * 4 + 0:h * 4 + 1], rss_b[:], ALU.add)
                    s.tt(fw_s[:, h * 4 + 1:h * 4 + 2], fw_s[:, h * 4 + 1:h * 4 + 2], rsl_b[:], ALU.add)
                    hs = slice(h * 256, (h + 1) * 256)
                    s.ts(o_tm[:, hs], sh_tm[:, hs], fw_s[:, h * 4 + 0:h * 4 + 1], None, ALU.mult)
                    s.stt(o_tm[:, hs], lo_tm[:, hs], fw_s[:, h * 4 + 1:h * 4 + 2], o_tm[:, hs], ALU.mult, ALU.add)
                    s.stt(o_tm[:, hs], dl_tm[:, hs], fw_s[:, h * 4 + 2:h * 4 + 3], o_tm[:, hs], ALU.mult, ALU.add)
                    s.stt(o_tm[:, hs], vd_tm[:, hs], fw_s[:, h * 4 + 3:h * 4 + 4], o_tm[:, hs], ALU.mult, ALU.add)
                    s.act(sq_scr[:, 0:256], o_tm[:, hs], AF.Square)
                    s.red(sc1[:, 0:1], sq_scr[:, 0:256], ALU.add)
                    s.ts(sc1[:, 0:1], sc1[:, 0:1], 1.0 / DH0, None, ALU.mult)
                    s.act(sc1[:, 0:1], sc1[:, 0:1], AF.Sqrt, bias=epsb)
                    s.recip(sc1[:, 0:1], sc1[:, 0:1])
                    s.ts(o_tm[:, hs], o_tm[:, hs], sc1[:, 0:1], None, ALU.mult)
                for j in range(8):
                    s.trans(R_tr, o_tm[:, j * 128:(j + 1) * 128], ident[:])
                    s.copy(oT_s[:, j * 512 + sub * 128: j * 512 + sub * 128 + 128], R_tr)
            for j in range(8):
                for kk in range(8):
                    s.mm(bank2[:, 0:512], opT_s[:, kk * 1024 + j * 128: kk * 1024 + j * 128 + 128],
                         oT_s[:, kk * 512:(kk + 1) * 512], start=(kk == 0), stop=(kk == 7))
                for sub in range(4):
                    s.copy(stg[:, 0:128], bank2[:, sub * 128:sub * 128 + 128])
                    s.trans(R_tr, stg[:, 0:128], ident[:])
                    s.copy(o_tm[:, j * 128:(j + 1) * 128], R_tr)
                    s.dma(attn_o[t0 + sub * 128:t0 + sub * 128 + 128, j * 128:(j + 1) * 128],
                          o_tm[:, j * 128:(j + 1) * 128])
        s.finalize(nc)
    print("p3 instrs:", len(s.q), "sem total:", s.cnt)
    return nc


def run_p3(inputs, r1_results, delta_list):
    consts = make_consts()
    fw1 = inputs["L0_attn_fuse_w1"]
    firs = inputs["L0_attn_fir_short_w"].reshape(1024, 5)
    firl = inputs["L0_attn_fir_long_w"].reshape(1024, 64)
    firs_t = np.ascontiguousarray(firs.reshape(8, 128, 5).transpose(1, 0, 2).reshape(128, 40))
    firl_t = np.ascontiguousarray(firl.reshape(8, 128, 64).transpose(1, 0, 2).reshape(128, 512))
    base = {
        "firs": firs_t,
        "firl": firl_t,
        "w1aT": np.ascontiguousarray(fw1[:, :1024].T),
        "w1bT": np.ascontiguousarray(fw1[:, 1024:].T),
        "b1c": np.ascontiguousarray(inputs["L0_attn_fuse_b1"].reshape(8, 128).T),
        "w2T": np.ascontiguousarray(inputs["L0_attn_fuse_w2"].T),
        "b2c": inputs["L0_attn_fuse_b2"].reshape(4, 1),
        "ltemp": inputs["L0_attn_logit_temp"].reshape(1, 1),
        "rss": inputs["L0_attn_res_scale_short"].reshape(1, 1),
        "rsl": inputs["L0_attn_res_scale_long"].reshape(1, 1),
        "opT": np.ascontiguousarray(inputs["L0_attn_o_proj_w"].T),
        "onwc": np.ascontiguousarray(np.tile(inputs["L0_attn_o_norm_w"], 4).reshape(8, 128).T),
        "c_ident": consts["ident"],
        "c_ones_row": consts["ones_row"],
        "c_ccols": consts["ccols"],
    }
    attn = [np.zeros((TPC, D), np.float32) for _ in range(NCORES)]
    for b in range(BPC):
        nc = build_p3(b)
        in_maps = []
        for c in range(NCORES):
            m = dict(base)
            m["vc"] = r1_results[c]["vc"]
            m["delta"] = delta_list[c]
            m["xnT"] = r1_results[c]["xnT"]
            in_maps.append(m)
        res = run_bass_kernel_spmd(nc, in_maps, core_ids=list(range(NCORES)))
        for c in range(NCORES):
            attn[c][b * 2048:(b + 1) * 2048] = res.results[c]["attn0"][b * 2048:(b + 1) * 2048]
    return attn


def build_p3b():
    nc = mk_nc()
    s = Ser()
    x = nc.dram_tensor("x", [TPC, D], F32, kind="ExternalInput").ap()
    xn = nc.dram_tensor("xn", [TPC, D], F32, kind="ExternalInput").ap()
    attn0 = nc.dram_tensor("attn0", [TPC, D], F32, kind="ExternalInput").ap()
    wgT = nc.dram_tensor("wgT", [D, 4096], F32, kind="ExternalInput").ap()
    wuT = nc.dram_tensor("wuT", [D, 4096], F32, kind="ExternalInput").ap()
    wdT = nc.dram_tensor("wdT", [4096, D], F32, kind="ExternalInput").ap()
    wrT = nc.dram_tensor("wrT", [D, 8], F32, kind="ExternalInput").ap()
    brc = nc.dram_tensor("brc", [8, 1], F32, kind="ExternalInput").ap()
    x1_o = nc.dram_tensor("x1", [TPC, D], F32, kind="ExternalOutput").ap()
    xn1_o = nc.dram_tensor("xn1", [TPC, D], F32, kind="ExternalOutput").ap()
    xn1T_o = nc.dram_tensor("xn1T", [D, TPC], F32, kind="ExternalOutput").ap()
    probs_o = nc.dram_tensor("probs", [TPC, 8], F32, kind="ExternalOutput").ap()
    g_d = nc.dram_tensor("g_d", [TPC, 4096], F32, kind="Internal").ap()
    xqT_d = nc.dram_tensor("xqT_d", [D, TPC], BF16, kind="Internal").ap()
    zq_d = nc.dram_tensor("zq_d", [TPC, 4096], BF16, kind="Internal").ap()
    dqx_d = nc.dram_tensor("dqx_d", [TPC, 1], F32, kind="Internal").ap()
    dqz_d = nc.dram_tensor("dqz_d", [TPC, 1], F32, kind="Internal").ap()

    with contextlib.ExitStack() as st:
        ident = sb(nc, st, "ident", [128, 128])
        ones_row = sb(nc, st, "ones_row", [1, 128])
        ones_col = sb(nc, st, "ones_col", [128, 1])
        ccols = sb(nc, st, "ccols", [128, 5])
        s.dma(ident[:], nc.dram_tensor("c_ident", [128, 128], F32, kind="ExternalInput").ap())
        s.dma(ones_row[:], nc.dram_tensor("c_ones_row", [1, 128], F32, kind="ExternalInput").ap())
        s.dma(ones_col[:], nc.dram_tensor("c_ones_col", [128, 1], F32, kind="ExternalInput").ap())
        s.dma(ccols[:], nc.dram_tensor("c_ccols", [128, 5], F32, kind="ExternalInput").ap())
        epsb = ccols[:, 0:1]

        ident_bf = sb(nc, st, "ident_bf", [128, 128], BF16)
        s.scopy(ident_bf[:], ident[:])
        wbuf = sb(nc, st, "wbuf", [128, 32768], BF16)
        zscr = sb(nc, st, "zscr", [128, 4096])
        zscr2 = sb(nc, st, "zscr2", [128, 4096])
        z_tm = sb(nc, st, "z_tm", [128, 4096])
        zq_bf = sb(nc, st, "zq_bf", [128, 4096], BF16)
        zqT_bf = sb(nc, st, "zqT_bf", [128, 4096], BF16)
        xt = sb(nc, st, "xt", [128, D])
        xnt = sb(nc, st, "xnt", [128, D])
        xq_bf = sb(nc, st, "xq_bf", [128, D], BF16)
        xqT_bf = sb(nc, st, "xqT_bf", [128, 8 * 512], BF16)
        macc = sb(nc, st, "macc", [128, 1])
        mg_b = sb(nc, st, "mg_b", [128, 1])
        mu_b = sb(nc, st, "mu_b", [128, 1])
        md_b = sb(nc, st, "md_b", [128, 1])
        sc_a = sb(nc, st, "sc_a", [128, 1])
        sc_b = sb(nc, st, "sc_b", [128, 1])
        dqx = sb(nc, st, "dqx", [128, 1])
        dqz = sb(nc, st, "dqz", [128, 1])
        g_s = sb(nc, st, "g_s", [128, 512])
        u_s = sb(nc, st, "u_s", [128, 512])
        wr_s = sb(nc, st, "wr_s", [128, 64])
        br_s = sb(nc, st, "br_s", [8, 1])
        pr_s = sb(nc, st, "pr_s", [8, 512])
        pb_s = sb(nc, st, "pb_s", [128, 8])
        xnT_blk = sb(nc, st, "xnT_blk", [128, 8 * 512])
        st1 = sb(nc, st, "st1", [128, 1])
        st2 = sb(nc, st, "st2", [128, 1])
        st3 = sb(nc, st, "st3", [128, 1])
        sone = sb(nc, st, "sone", [1, 1])

        bankA = ps(nc, st, "bankA", [128, 512])
        bankB = ps(nc, st, "bankB", [128, 512])
        bankT = ps(nc, st, "bankT", [128, 256], BF16)
        bankS = ps(nc, st, "bankS", [128, 512])
        R_one = bankS[0:1, 0:1]

        def wquant(srcT, n_in_tiles, out_w, m_bcast):
            for j in range(n_in_tiles):
                s.dma(zscr[:, 0:out_w], srcT[j * 128:(j + 1) * 128, :])
                s.red(sc_a[:], zscr[:, 0:out_w], ALU.add, absval=True)
                if j == 0:
                    s.copy(macc[:], sc_a[:])
                else:
                    s.tt(macc[:], macc[:], sc_a[:], ALU.add)
            s.mm(R_one, ones_col[:], macc[:], start=True, stop=True)
            s.ts(sone[:], R_one, 1.0 / (n_in_tiles * 128 * out_w), 1e-5, ALU.mult, ALU.max)
            s.mm(bankS[:, 1:2], ones_row[:], sone[:], start=True, stop=True)
            s.copy(m_bcast[:], bankS[:, 1:2])
            s.recip(sc_b[:], m_bcast[:])
            for j in range(n_in_tiles):
                s.dma(zscr[:, 0:out_w], srcT[j * 128:(j + 1) * 128, :])
                s.emit("scalar", lambda e, o=zscr2, i=zscr, w=out_w, sc=sc_b:
                       e.activation(o[:, 0:w], i[:, 0:w], AF.Abs, scale=sc[:, 0:1]), 1)
                s.ts(z_tm[:, 0:out_w], zscr2[:, 0:out_w], 0.5, None, ALU.is_ge)
                s.act(zscr2[:, 0:out_w], zscr[:, 0:out_w], AF.Sign)
                s.tt(zscr[:, 0:out_w], z_tm[:, 0:out_w], zscr2[:, 0:out_w], ALU.mult)
                s.scopy(wbuf[:, j * out_w:(j + 1) * out_w], zscr[:, 0:out_w])

        def actquant(src_t, width, dq_col, out_bf, ge_scr):
            s.red(sc_a[:], src_t, ALU.max, absval=True)
            s.ts(sc_a[:], sc_a[:], 1e-5, None, ALU.max)
            s.ts(dq_col[:], sc_a[:], 1.0 / 127.0, None, ALU.mult)
            s.recip(sc_b[:], sc_a[:])
            s.ts(sc_b[:], sc_b[:], 127.0, None, ALU.mult)
            s.emit("scalar", lambda e, o=zscr, i=src_t, w=width, sc=sc_b:
                   e.activation(o[:, 0:w], i, AF.Identity, bias=ccols[:, 4:5], scale=sc[:, 0:1]), 1)
            s.copy(zscr2[:, 0:width], zscr[:, 0:width])
            for bit in (128.0, 64.0, 32.0, 16.0, 8.0, 4.0, 2.0, 1.0):
                s.ts(ge_scr[:, 0:width], zscr2[:, 0:width], bit, None, ALU.is_ge)
                s.stt(zscr2[:, 0:width], ge_scr[:, 0:width], -bit, zscr2[:, 0:width], ALU.mult, ALU.add)
            s.tt(zscr[:, 0:width], zscr[:, 0:width], zscr2[:, 0:width], ALU.subtract)
            s.ts(zscr[:, 0:width], zscr[:, 0:width], -128.0, None, ALU.add)
            s.scopy(out_bf, zscr[:, 0:width])

        wquant(wgT, 8, 4096, mg_b)
        for kk in range(8):
            s.dma(wr_s[:, kk * 8:(kk + 1) * 8], wrT[kk * 128:(kk + 1) * 128, :])
        s.dma(br_s[:], brc)
        for blk in range(8):
            t0 = blk * 512
            for sub in range(4):
                r0 = t0 + sub * 128
                s.dma(xnt[:], xn[r0:r0 + 128, :])
                actquant(xnt[:], D, dqx, xq_bf[:], xnt)
                s.dma(dqx_d[r0:r0 + 128, :], dqx[:])
                for j in range(8):
                    s.trans(bankT[:, 0:128], xq_bf[:, j * 128:(j + 1) * 128], ident_bf[:])
                    s.copy(xqT_bf[:, j * 512 + sub * 128: j * 512 + sub * 128 + 128], bankT[:, 0:128])
                s.tt(sc_a[:], dqx[:], mg_b[:], ALU.mult)
                for ns in range(8):
                    for kk in range(8):
                        s.mm(bankA[:, 0:512], xqT_bf[:, kk * 512 + sub * 128: kk * 512 + sub * 128 + 128],
                             wbuf[:, kk * 4096 + ns * 512: kk * 4096 + (ns + 1) * 512],
                             start=(kk == 0), stop=(kk == 7), chain=True)
                    s.ts(g_s[:], bankA[:, 0:512], sc_a[:, 0:1], None, ALU.mult)
                    s.act(z_tm[:, ns * 512:(ns + 1) * 512], g_s[:], AF.Silu)
                s.dma(g_d[r0:r0 + 128, :], z_tm[:])
            for j in range(8):
                s.dma(xqT_d[j * 128:(j + 1) * 128, t0:t0 + 512], xqT_bf[:, j * 512:(j + 1) * 512])

        wquant(wuT, 8, 4096, mu_b)
        for blk in range(8):
            t0 = blk * 512
            for j in range(8):
                s.dma(xqT_bf[:, j * 512:(j + 1) * 512], xqT_d[j * 128:(j + 1) * 128, t0:t0 + 512])
            for sub in range(4):
                r0 = t0 + sub * 128
                s.dma(dqx[:], dqx_d[r0:r0 + 128, :])
                s.dma(z_tm[:], g_d[r0:r0 + 128, :])
                s.tt(sc_a[:], dqx[:], mu_b[:], ALU.mult)
                for ns in range(8):
                    for kk in range(8):
                        s.mm(bankA[:, 0:512], xqT_bf[:, kk * 512 + sub * 128: kk * 512 + sub * 128 + 128],
                             wbuf[:, kk * 4096 + ns * 512: kk * 4096 + (ns + 1) * 512],
                             start=(kk == 0), stop=(kk == 7), chain=True)
                    s.ts(u_s[:], bankA[:, 0:512], sc_a[:, 0:1], None, ALU.mult)
                    s.tt(z_tm[:, ns * 512:(ns + 1) * 512], z_tm[:, ns * 512:(ns + 1) * 512],
                         u_s[:], ALU.mult)
                actquant(z_tm[:], 4096, dqz, zq_bf[:], z_tm)
                s.dma(zq_d[r0:r0 + 128, :], zq_bf[:])
                s.dma(dqz_d[r0:r0 + 128, :], dqz[:])

        wquant(wdT, 32, 1024, md_b)
        for blk in range(8):
            t0 = blk * 512
            for sub in range(4):
                r0 = t0 + sub * 128
                s.dma(zq_bf[:], zq_d[r0:r0 + 128, :])
                s.dma(dqz[:], dqz_d[r0:r0 + 128, :])
                for j in range(32):
                    s.trans(bankT[:, 0:128], zq_bf[:, j * 128:(j + 1) * 128], ident_bf[:])
                    s.copy(zqT_bf[:, j * 128:(j + 1) * 128], bankT[:, 0:128])
                s.tt(sc_a[:], dqz[:], md_b[:], ALU.mult)
                s.dma(xt[:], x[r0:r0 + 128, :])
                s.dma(xnt[:], attn0[r0:r0 + 128, :])
                s.tt(xt[:], xt[:], xnt[:], ALU.add)
                for half in range(2):
                    for kk in range(32):
                        s.mm(bankA[:, 0:512], zqT_bf[:, kk * 128:(kk + 1) * 128],
                             wbuf[:, kk * 1024 + half * 512: kk * 1024 + half * 512 + 512],
                             start=(kk == 0), stop=(kk == 31), chain=True)
                    s.ts(g_s[:], bankA[:, 0:512], sc_a[:, 0:1], None, ALU.mult)
                    s.tt(xt[:, half * 512:(half + 1) * 512], xt[:, half * 512:(half + 1) * 512],
                         g_s[:], ALU.add)
                s.dma(x1_o[r0:r0 + 128, :], xt[:])
                s.red(st1[:], xt[:], ALU.add)
                s.ttr(st2[:], xt[:], xt[:], zscr[:, 0:1024])
                s.ts(st1[:], st1[:], 1.0 / D, None, ALU.mult)
                s.ts(st2[:], st2[:], 1.0 / D, None, ALU.mult)
                s.emit("scalar", lambda e, o=st3, i=st1: e.square(o[:], i[:]), 1)
                s.tt(st2[:], st2[:], st3[:], ALU.subtract)
                s.act(st3[:], st2[:], AF.Sqrt, bias=epsb)
                s.recip(st3[:], st3[:])
                s.emit("vector", lambda e, o=xnt, i=xt, m=st1, r=st3: e.tensor_scalar(
                    o[:], i[:], m[:], r[:], ALU.subtract, ALU.mult), 1)
                s.dma(xn1_o[r0:r0 + 128, :], xnt[:])
                for j in range(8):
                    s.trans(bankS[:, 0:128], xnt[:, j * 128:(j + 1) * 128], ident[:])
                    s.copy(xnT_blk[:, j * 512 + sub * 128: j * 512 + sub * 128 + 128], bankS[:, 0:128])
            for j in range(8):
                s.dma(xn1T_o[j * 128:(j + 1) * 128, t0:t0 + 512], xnT_blk[:, j * 512:(j + 1) * 512])
            for kk in range(8):
                s.mm(bankA[0:8, 0:512], wr_s[:, kk * 8:(kk + 1) * 8],
                     xnT_blk[:, kk * 512:(kk + 1) * 512], start=(kk == 0), stop=(kk == 7))
            s.ts(pr_s[:], bankA[0:8, 0:512], br_s[:, 0:1], None, ALU.add)
            for sub in range(4):
                s.trans(bankS[0:128, 0:8], pr_s[:, sub * 128:(sub + 1) * 128], ident[0:8, 0:8])
                s.copy(pb_s[:], bankS[0:128, 0:8])
                s.red(st1[:], pb_s[:], ALU.max, negate=True)
                s.emit("scalar", lambda e, o=pb_s, bcol=st1: e.activation(
                    o[:], o[:], AF.Exp, bias=bcol[:, 0:1]), 1)
                s.red(st1[:], pb_s[:], ALU.add)
                s.recip(st1[:], st1[:])
                s.ts(pb_s[:], pb_s[:], st1[:, 0:1], None, ALU.mult)
                s.dma(probs_o[t0 + sub * 128: t0 + sub * 128 + 128, :], pb_s[:])
        s.finalize(nc)
    print("p3b instrs:", len(s.q), "sem total:", s.cnt)
    return nc


def run_p3b(inputs, xfull, r1_results, attn_list):
    consts = make_consts()
    nc = build_p3b()
    base = {
        "wgT": np.ascontiguousarray(inputs["L0_ffn_w_gate"].T),
        "wuT": np.ascontiguousarray(inputs["L0_ffn_w_up"].T),
        "wdT": np.ascontiguousarray(inputs["L0_ffn_w_down"].T),
        "wrT": np.ascontiguousarray(inputs["L1_ffn_W_router"].T),
        "brc": inputs["L1_ffn_b_router"].reshape(8, 1),
        "c_ident": consts["ident"],
        "c_ones_row": consts["ones_row"],
        "c_ones_col": consts["ones_col"],
        "c_ccols": consts["ccols"],
    }
    in_maps = []
    for c in range(NCORES):
        m = dict(base)
        m["x"] = np.ascontiguousarray(xfull[c * TPC:(c + 1) * TPC])
        m["xn"] = r1_results[c]["xn"]
        m["attn0"] = attn_list[c]
        in_maps.append(m)
    res = run_bass_kernel_spmd(nc, in_maps, core_ids=list(range(NCORES)))
    return res.results


def build_p4(b):
    nc = mk_nc()
    s = Ser()
    xn1T = nc.dram_tensor("xn1T", [D, TPC], F32, kind="ExternalInput").ap()
    Wqa = nc.dram_tensor("Wqa", [D, 64], F32, kind="ExternalInput").ap()
    Wqb = nc.dram_tensor("Wqb", [64, D], F32, kind="ExternalInput").ap()
    Wkva = nc.dram_tensor("Wkva", [D, 32], F32, kind="ExternalInput").ap()
    Wkvb = nc.dram_tensor("Wkvb", [32, 512], F32, kind="ExternalInput").ap()
    WoutT = nc.dram_tensor("WoutT", [D, D], F32, kind="ExternalInput").ap()
    boutc = nc.dram_tensor("boutc", [128, 8], F32, kind="ExternalInput").ap()
    cqh = nc.dram_tensor("cqh", [SEQ, D], F32, kind="ExternalInput").ap()
    cql = nc.dram_tensor("cql", [SEQ, D], F32, kind="ExternalInput").ap()
    sqh = nc.dram_tensor("sqh", [SEQ, D], F32, kind="ExternalInput").ap()
    sql = nc.dram_tensor("sql", [SEQ, D], F32, kind="ExternalInput").ap()
    ckh = nc.dram_tensor("ckh", [SEQ, 256], F32, kind="ExternalInput").ap()
    ckl = nc.dram_tensor("ckl", [SEQ, 256], F32, kind="ExternalInput").ap()
    skh = nc.dram_tensor("skh", [SEQ, 256], F32, kind="ExternalInput").ap()
    skl = nc.dram_tensor("skl", [SEQ, 256], F32, kind="ExternalInput").ap()
    attn_o = nc.dram_tensor("attn1", [TPC, D], F32, kind="ExternalOutput").ap()

    with contextlib.ExitStack() as st:
        ident = sb(nc, st, "ident", [128, 128])
        causal = sb(nc, st, "causal", [128, 128])
        s.dma(ident[:], nc.dram_tensor("c_ident", [128, 128], F32, kind="ExternalInput").ap())
        s.dma(causal[:], nc.dram_tensor("c_causal_add", [128, 128], F32, kind="ExternalInput").ap())
        wqa_s = sb(nc, st, "wqa_s", [128, 8 * 64])
        wqb_s = sb(nc, st, "wqb_s", [64, D])
        wkva_s = sb(nc, st, "wkva_s", [128, 8 * 32])
        wkvb_s = sb(nc, st, "wkvb_s", [32, 512])
        wout_s = sb(nc, st, "wout_s", [128, 8 * 1024], BF16)
        wstg = sb(nc, st, "wstg", [128, 1024])
        bout_s = sb(nc, st, "bout_s", [128, 8])
        qmidT = sb(nc, st, "qmidT", [64, SEQ])
        kvmidT = sb(nc, st, "kvmidT", [32, SEQ])
        ident_bf = sb(nc, st, "ident_bf", [128, 128], BF16)
        s.scopy(ident_bf[:], ident[:])
        qT = sb(nc, st, "qT", [64, 16 * SEQ], BF16)
        kT = sb(nc, st, "kT", [64, 4 * SEQ], BF16)
        V = sb(nc, st, "V", [128, 16 * 256], BF16)
        q_bf = sb(nc, st, "q_bf", [128, D], BF16)
        p_bf = sb(nc, st, "p_bf", [128, 128], BF16)
        pT_bf = sb(nc, st, "pT_bf", [128, 128], BF16)
        o_bf = sb(nc, st, "o_bf", [128, D], BF16)
        q_tm = sb(nc, st, "q_tm", [128, D])
        qsw = sb(nc, st, "qsw", [128, D])
        ctab = sb(nc, st, "ctab", [128, D])
        stab = sb(nc, st, "stab", [128, D])
        kv_tm = sb(nc, st, "kv_tm", [128, 512])
        ksw = sb(nc, st, "ksw", [128, 256])
        scores = sb(nc, st, "scores", [128, SEQ])
        out_tm = sb(nc, st, "out_tm", [128, D])
        outT = sb(nc, st, "outT", [128, 8 * 512], BF16)
        o_st = sb(nc, st, "o_st", [128, 512])
        nmax = sb(nc, st, "nmax", [128, 1])
        rsum = sb(nc, st, "rsum", [128, 1])

        bankA = ps(nc, st, "bankA", [128, 512])
        bankB = ps(nc, st, "bankB", [128, 512])
        bankC = ps(nc, st, "bankC", [128, 512])
        bankT = ps(nc, st, "bankT", [128, 256], BF16)
        R_tr = bankC[:, 0:128]
        R_o = bankB[:, 0:64]

        c0 = b * SEQ
        for kk in range(8):
            s.dma(wqa_s[:, kk * 64:(kk + 1) * 64], Wqa[kk * 128:(kk + 1) * 128, :])
            s.dma(wkva_s[:, kk * 32:(kk + 1) * 32], Wkva[kk * 128:(kk + 1) * 128, :])
            s.dma(wstg[:], WoutT[kk * 128:(kk + 1) * 128, :])
            s.scopy(wout_s[:, kk * 1024:(kk + 1) * 1024], wstg[:])
        s.dma(wqb_s[:], Wqb)
        s.dma(wkvb_s[:], Wkvb)
        s.dma(bout_s[:], boutc)

        xnT_blk = sb(nc, st, "xnT_blk", [128, 8 * 512])
        for blk in range(4):
            t0 = blk * 512
            for kk in range(8):
                s.dma(xnT_blk[:, kk * 512:(kk + 1) * 512],
                      xn1T[kk * 128:(kk + 1) * 128, c0 + t0:c0 + t0 + 512])
            for kk in range(8):
                s.mm(bankA[0:64, 0:512], wqa_s[:, kk * 64:(kk + 1) * 64],
                     xnT_blk[:, kk * 512:(kk + 1) * 512],
                     start=(kk == 0), stop=(kk == 7))
            s.copy(qmidT[:, t0:t0 + 512], bankA[0:64, 0:512])
            for kk in range(8):
                s.mm(bankB[0:32, 0:512], wkva_s[:, kk * 32:(kk + 1) * 32],
                     xnT_blk[:, kk * 512:(kk + 1) * 512],
                     start=(kk == 0), stop=(kk == 7))
            s.copy(kvmidT[:, t0:t0 + 512], bankB[0:32, 0:512])

        for ti in range(16):
            t0 = ti * 128
            for half in range(2):
                s.mm(bankA[:, 0:512], qmidT[:, t0:t0 + 128],
                     wqb_s[:, half * 512:(half + 1) * 512], start=True, stop=True)
                s.copy(q_tm[:, half * 512:(half + 1) * 512], bankA[:, 0:512])
            s.dma(ctab[:], cqh[t0:t0 + 128, :])
            s.dma(stab[:], cql[t0:t0 + 128, :])
            s.tt(ctab[:], ctab[:], stab[:], ALU.add)
            s.dma(stab[:], sqh[t0:t0 + 128, :])
            s.dma(qsw[:], sql[t0:t0 + 128, :])
            s.tt(stab[:], stab[:], qsw[:], ALU.add)
            q3 = q_tm[:].rearrange("p (f two) -> p f two", two=2)
            w3 = qsw[:].rearrange("p (f two) -> p f two", two=2)
            s.copy(w3[:, :, 0], q3[:, :, 1])
            s.copy(w3[:, :, 1], q3[:, :, 0])
            s.tt(q_tm[:], q_tm[:], ctab[:], ALU.mult)
            s.tt(qsw[:], qsw[:], stab[:], ALU.mult)
            s.tt(q_tm[:], q_tm[:], qsw[:], ALU.add)
            s.scopy(q_bf[:], q_tm[:])
            for j in range(8):
                s.trans(bankT[:, 0:128], q_bf[:, j * 128:(j + 1) * 128], ident_bf[:])
                s.copy(qT[:, (2 * j) * SEQ + t0: (2 * j) * SEQ + t0 + 128], bankT[0:64, 0:128])
                s.copy(qT[:, (2 * j + 1) * SEQ + t0: (2 * j + 1) * SEQ + t0 + 128],
                       bankT[64:128, 0:128])
            s.mm(bankA[:, 0:512], kvmidT[:, t0:t0 + 128], wkvb_s[:], start=True, stop=True)
            s.copy(kv_tm[:], bankA[:, 0:512])
            s.dma(ctab[:, 0:256], ckh[t0:t0 + 128, :])
            s.dma(stab[:, 0:256], ckl[t0:t0 + 128, :])
            s.tt(ctab[:, 0:256], ctab[:, 0:256], stab[:, 0:256], ALU.add)
            s.dma(stab[:, 0:256], skh[t0:t0 + 128, :])
            s.dma(ksw[:], skl[t0:t0 + 128, :])
            s.tt(stab[:, 0:256], stab[:, 0:256], ksw[:], ALU.add)
            k3 = kv_tm[:, 0:256].rearrange("p (f two) -> p f two", two=2)
            ks3 = ksw[:].rearrange("p (f two) -> p f two", two=2)
            s.copy(ks3[:, :, 0], k3[:, :, 1])
            s.copy(ks3[:, :, 1], k3[:, :, 0])
            s.tt(kv_tm[:, 0:256], kv_tm[:, 0:256], ctab[:, 0:256], ALU.mult)
            s.tt(ksw[:], ksw[:], stab[:, 0:256], ALU.mult)
            s.tt(kv_tm[:, 0:256], kv_tm[:, 0:256], ksw[:], ALU.add)
            s.scopy(q_bf[:, 0:256], kv_tm[:, 0:256])
            for half in range(2):
                s.trans(bankT[:, 0:128], q_bf[:, half * 128:half * 128 + 128], ident_bf[:])
                s.copy(kT[:, (2 * half) * SEQ + t0: (2 * half) * SEQ + t0 + 128], bankT[0:64, 0:128])
                s.copy(kT[:, (2 * half + 1) * SEQ + t0: (2 * half + 1) * SEQ + t0 + 128],
                       bankT[64:128, 0:128])
            s.scopy(V[:, ti * 256:(ti + 1) * 256], kv_tm[:, 256:512])

        for qi in range(16):
            nb = (qi * 128 + 128 + 511) // 512
            L = nb * 512
            for h in range(16):
                kvh = h // 4
                for kb in range(nb):
                    s.mm(bankA[:, 0:512],
                         qT[:, h * SEQ + qi * 128: h * SEQ + qi * 128 + 128],
                         kT[:, kvh * SEQ + kb * 512: kvh * SEQ + (kb + 1) * 512],
                         start=True, stop=True)
                    s.act(scores[:, kb * 512:(kb + 1) * 512], bankA[:, 0:512],
                          AF.Identity, scale=0.125)
                s.tt(scores[:, qi * 128:qi * 128 + 128], scores[:, qi * 128:qi * 128 + 128],
                     causal[:], ALU.add)
                if qi * 128 + 128 < L:
                    s.memset(scores[:, qi * 128 + 128:L], -1e30)
                s.red(nmax[:], scores[:, 0:L], ALU.max, negate=True)
                s.emit("scalar", lambda e, o=scores, LL=L, bcol=nmax: e.activation(
                    o[:, 0:LL], o[:, 0:LL], AF.Exp, bias=bcol[:, 0:1]), 1)
                s.red(rsum[:], scores[:, 0:L], ALU.add)
                s.recip(rsum[:], rsum[:])
                for kt in range(qi + 1):
                    s.scopy(p_bf[:], scores[:, kt * 128:(kt + 1) * 128])
                    s.trans(bankT[:, 0:128], p_bf[:], ident_bf[:])
                    s.copy(pT_bf[:], bankT[:, 0:128])
                    s.mm(R_o, pT_bf[:], V[:, kt * 256 + kvh * 64: kt * 256 + kvh * 64 + 64],
                         start=(kt == 0), stop=(kt == qi))
                s.ts(out_tm[:, h * 64:(h + 1) * 64], R_o, rsum[:, 0:1], None, ALU.mult)
            s.scopy(o_bf[:], out_tm[:])
            for j in range(8):
                s.trans(bankT[:, 0:128], o_bf[:, j * 128:(j + 1) * 128], ident_bf[:])
                s.copy(outT[:, j * 512 + (qi % 4) * 128: j * 512 + (qi % 4) * 128 + 128], bankT[:, 0:128])
            if qi % 4 == 3:
                blk0 = (qi // 4) * 512
                for j in range(8):
                    for kk in range(8):
                        s.mm(bankB[:, 0:512], wout_s[:, kk * 1024 + j * 128: kk * 1024 + j * 128 + 128],
                             outT[:, kk * 512:(kk + 1) * 512], start=(kk == 0), stop=(kk == 7))
                    s.ts(o_st[:], bankB[:, 0:512], bout_s[:, j:j + 1], None, ALU.add)
                    for sub in range(4):
                        s.trans(R_tr, o_st[:, sub * 128:sub * 128 + 128], ident[:])
                        s.copy(q_tm[:, j * 128:(j + 1) * 128], R_tr)
                        s.dma(attn_o[c0 + blk0 + sub * 128: c0 + blk0 + sub * 128 + 128,
                                     j * 128:(j + 1) * 128], q_tm[:, j * 128:(j + 1) * 128])
        s.finalize(nc)
    print("p4 instrs:", len(s.q), "sem total:", s.cnt)
    return nc


def _rope_tables(n, d_head, reps):
    inv = 1.0 / (10000.0 ** (np.arange(0, d_head, 2, dtype=np.float64) / d_head))
    t = np.arange(n, dtype=np.float64)
    fr = np.outer(t, inv)
    cos = np.cos(fr); sin = np.sin(fr)
    def expand(tab, sign_first):
        out = np.zeros((n, d_head), np.float64)
        out[:, 0::2] = tab * (-1.0 if sign_first else 1.0)
        out[:, 1::2] = tab
        return np.tile(out, (1, reps))
    cI = expand(cos, False)
    sI = expand(sin, True)
    def ds(a):
        hi = a.astype(np.float32)
        lo = (a - hi.astype(np.float64)).astype(np.float32)
        return hi, lo
    ch, cl = ds(cI); sh, sl = ds(sI)
    return ch, cl, sh, sl


def run_p4(inputs, r3b_results):
    consts = make_consts()
    cqh, cql, sqh, sql = _rope_tables(SEQ, 64, 16)
    ckh, ckl, skh, skl = _rope_tables(SEQ, 64, 4)
    kvb = inputs["L1_attn_Wkvb"]
    perm = np.array([kvh * 128 + d * 2 + j for j in range(2) for kvh in range(4) for d in range(64)])
    base = {
        "Wqa": inputs["L1_attn_Wqa"],
        "Wqb": inputs["L1_attn_Wqb"],
        "Wkva": inputs["L1_attn_Wkva"],
        "Wkvb": np.ascontiguousarray(kvb[:, perm]),
        "WoutT": np.ascontiguousarray(inputs["L1_attn_Wout"]),
        "boutc": np.ascontiguousarray(inputs["L1_attn_bout"].reshape(8, 128).T),
        "cqh": cqh, "cql": cql, "sqh": sqh, "sql": sql,
        "ckh": ckh, "ckl": ckl, "skh": skh, "skl": skl,
        "c_ident": consts["ident"],
        "c_causal_add": consts["causal_add"],
    }
    attn = [np.zeros((TPC, D), np.float32) for _ in range(NCORES)]
    for b in range(BPC):
        nc = build_p4(b)
        in_maps = []
        for c in range(NCORES):
            m = dict(base)
            m["xn1T"] = r3b_results[c]["xn1T"]
            in_maps.append(m)
        res = run_bass_kernel_spmd(nc, in_maps, core_ids=list(range(NCORES)))
        for c in range(NCORES):
            attn[c][b * SEQ:(b + 1) * SEQ] = res.results[c]["attn1"][b * SEQ:(b + 1) * SEQ]
    return attn


MOE_CAP = 5120
MOE_PAD = NTOK + MOE_CAP + 128

def build_p5(e):
    nc = mk_nc()
    s = Ser()
    xn1f = nc.dram_tensor("xn1f", [NTOK, D], F32, kind="ExternalInput").ap()
    probsf = nc.dram_tensor("probsf", [NTOK, 8], F32, kind="ExternalInput").ap()
    w1 = nc.dram_tensor("w1", [D, 4096], F32, kind="ExternalInput").ap()
    w2 = nc.dram_tensor("w2", [4096, D], F32, kind="ExternalInput").ap()
    tvals = nc.dram_tensor("tvals", [128, 256], U32, kind="ExternalInput").ap()
    tvals_f = nc.dram_tensor("tvals_f", [128, 256], F32, kind="ExternalInput").ap()
    prefill = nc.dram_tensor("prefill", [128, 40], U32, kind="ExternalInput").ap()
    su128 = nc.dram_tensor("c_su_mask", [128, 128], F32, kind="ExternalInput").ap()
    moe_o = nc.dram_tensor("moe", [MOE_PAD, D], F32, kind="ExternalOutput").ap()
    idxbuf = nc.dram_tensor("idxbuf", [MOE_PAD, 1], U32, kind="Internal").ap()
    gatebuf = nc.dram_tensor("gatebuf", [MOE_PAD, 1], F32, kind="Internal").ap()

    with contextlib.ExitStack() as st:
        ident = sb(nc, st, "ident", [128, 128])
        ident_bf = sb(nc, st, "ident_bf", [128, 128], BF16)
        s.dma(ident[:], nc.dram_tensor("c_ident", [128, 128], F32, kind="ExternalInput").ap())
        s.scopy(ident_bf[:], ident[:])
        su_s = sb(nc, st, "su_s", [128, 128])
        s.dma(su_s[:], su128)
        P_s = sb(nc, st, "P_s", [128, 2048])
        m1 = sb(nc, st, "m1", [128, 256])
        m2 = sb(nc, st, "m2", [128, 256])
        w2_s = sb(nc, st, "w2scr", [128, 2048])
        eqt = sb(nc, st, "eqt", [128, 256])
        tmp = sb(nc, st, "tmp", [128, 256])
        mask = sb(nc, st, "mask", [128, 256])
        gate = sb(nc, st, "gate", [128, 256])
        zeros = sb(nc, st, "zeros", [128, 256])
        rowcum = sb(nc, st, "rowcum", [128, 256])
        pfx = sb(nc, st, "pfx", [128, 1])
        cglob = sb(nc, st, "cglob", [128, 256])
        keep = sb(nc, st, "keep", [128, 256])
        slotp = sb(nc, st, "slotp", [128, 256])
        slot_u = sb(nc, st, "slot_u", [128, 256], U32)
        tv_u = sb(nc, st, "tv_u", [128, 256], U32)
        tv_f = sb(nc, st, "tv_f", [128, 256])
        idx_u = sb(nc, st, "idx_u", [128, 1], U32)
        idxc_f = sb(nc, st, "idxc_f", [128, 1])
        idxc_u = sb(nc, st, "idxc_u", [128, 1], U32)
        gcol = sb(nc, st, "gcol", [128, 1])
        w1_bf = sb(nc, st, "w1_bf", [128, 8 * 4096], BF16)
        w2_bf = sb(nc, st, "w2_bf", [128, 32 * 1024], BF16)
        stg = sb(nc, st, "stg", [128, 4096])
        xg_t = sb(nc, st, "xg_t", [128, D])
        xg_bf = sb(nc, st, "xg_bf", [128, D], BF16)
        xgT_bf = sb(nc, st, "xgT_bf", [128, 8 * 128], BF16)
        h_bf = sb(nc, st, "h_bf", [128, 4096], BF16)
        hT_bf = sb(nc, st, "hT_bf", [128, 32 * 128], BF16)
        out_t = sb(nc, st, "out_t", [128, D])
        bankA = ps(nc, st, "bankA", [128, 512])
        bankT = ps(nc, st, "bankT", [128, 256], BF16)
        bankP = ps(nc, st, "bankP", [128, 512])

        for j in range(8):
            s.dma(stg[:], w1[j * 128:(j + 1) * 128, :])
            s.scopy(w1_bf[:, j * 4096:(j + 1) * 4096], stg[:])
        for j in range(32):
            s.dma(stg[:, 0:1024], w2[j * 128:(j + 1) * 128, :])
            s.scopy(w2_bf[:, j * 1024:(j + 1) * 1024], stg[:, 0:1024])

        s.dma(P_s[:], probsf.rearrange("(p f) e -> p (f e)", p=128))
        P3 = P_s[:].rearrange("p (f e) -> p f e", e=8)
        s.red(m1[:], P3, ALU.max)
        W3 = w2_s[:].rearrange("p (f e) -> p f e", e=8)
        for ee in range(8):
            s.tt(eqt[:], P3[:, :, ee], m1[:], ALU.is_ge)
            s.ts(tmp[:], P3[:, :, ee], 1.0, None, ALU.add)
            s.tt(tmp[:], tmp[:], eqt[:], ALU.mult)
            s.tt(W3[:, :, ee], P3[:, :, ee], tmp[:], ALU.subtract)
        s.red(m2[:], W3, ALU.max)
        s.tt(mask[:], P3[:, :, e], m2[:], ALU.is_ge)
        s.tt(gate[:], P3[:, :, e], mask[:], ALU.mult)
        s.memset(zeros[:], 0.0)
        s.scan_add(rowcum[:], mask[:], zeros[:])
        s.mm(bankP[:, 0:1], su_s[:], rowcum[:, 255:256], start=True, stop=True)
        s.copy(pfx[:], bankP[:, 0:1])
        s.ts(cglob[:], rowcum[:], pfx[:, 0:1], None, ALU.add)
        s.ts(keep[:], cglob[:], float(MOE_CAP), None, ALU.is_le)
        s.tt(keep[:], keep[:], mask[:], ALU.mult)
        s.dma(tv_f[:], tvals_f)
        s.ts(slotp[:], cglob[:], -1.0, None, ALU.add)
        s.ts(tmp[:], tv_f[:], float(MOE_CAP), None, ALU.add)
        s.tt(slotp[:], slotp[:], tmp[:], ALU.subtract)
        s.tt(slotp[:], slotp[:], keep[:], ALU.mult)
        s.tt(slotp[:], slotp[:], tmp[:], ALU.add)
        s.copy(slot_u[:], slotp[:])
        s.dma(tv_u[:, 0:40], prefill)
        s.dma(idxbuf[0:MOE_CAP, 0:1].rearrange("(p f) one -> p (f one)", p=128),
              tv_u[:, 0:40])
        s.dma(gatebuf[0:MOE_CAP, 0:1].rearrange("(p f) one -> p (f one)", p=128),
              zeros[:, 0:40])
        s.dma(tv_u[:], tvals)
        for f in range(256):
            s.idma_scatter(idxbuf, bass.IndirectOffsetOnAxis(slot_u[:, f:f + 1], 0),
                           tv_u[:, f:f + 1])
        for f in range(256):
            s.idma_scatter(gatebuf, bass.IndirectOffsetOnAxis(slot_u[:, f:f + 1], 0),
                           gate[:, f:f + 1])
        for g in range(MOE_CAP // 128):
            s.dma(idx_u[:], idxbuf[g * 128:(g + 1) * 128, :])
            s.copy(idxc_f[:], idx_u[:])
            s.ts(idxc_f[:], idxc_f[:], float(NTOK - 1), None, ALU.min)
            s.copy(idxc_u[:], idxc_f[:])
            s.dma(gcol[:], gatebuf[g * 128:(g + 1) * 128, :])
            s.idma_gather(xg_t[:], xn1f, bass.IndirectOffsetOnAxis(idxc_u[:, 0:1], 0))
            s.scopy(xg_bf[:], xg_t[:])
            for j in range(8):
                s.trans(bankT[:, 0:128], xg_bf[:, j * 128:(j + 1) * 128], ident_bf[:])
                s.copy(xgT_bf[:, j * 128:(j + 1) * 128], bankT[:, 0:128])
            for ns in range(8):
                for kk in range(8):
                    s.mm(bankA[:, 0:512], xgT_bf[:, kk * 128:(kk + 1) * 128],
                         w1_bf[:, kk * 4096 + ns * 512: kk * 4096 + (ns + 1) * 512],
                         start=(kk == 0), stop=(kk == 7), chain=True)
                s.emit("scalar", lambda e2, o=h_bf, nss=ns, pp=bankA: e2.activation(
                    o[:, nss * 512:(nss + 1) * 512], pp[:, 0:512], AF.Relu), 1)
            for j in range(32):
                s.trans(bankT[:, 0:128], h_bf[:, j * 128:(j + 1) * 128], ident_bf[:])
                s.copy(hT_bf[:, j * 128:(j + 1) * 128], bankT[:, 0:128])
            for half in range(2):
                for kk in range(32):
                    s.mm(bankA[:, 0:512], hT_bf[:, kk * 128:(kk + 1) * 128],
                         w2_bf[:, kk * 1024 + half * 512: kk * 1024 + half * 512 + 512],
                         start=(kk == 0), stop=(kk == 31), chain=True)
                s.ts(out_t[:, half * 512:(half + 1) * 512], bankA[:, 0:512],
                     gcol[:, 0:1], None, ALU.mult)
            s.idma_scatter(moe_o, bass.IndirectOffsetOnAxis(idx_u[:, 0:1], 0), out_t[:])
        s.finalize(nc)
    print("p5 instrs:", len(s.q), "sem total:", s.cnt)
    return nc


def build_p6():
    nc = mk_nc()
    s = Ser()
    x1 = nc.dram_tensor("x1", [TPC, D], F32, kind="ExternalInput").ap()
    attn1 = nc.dram_tensor("attn1", [TPC, D], F32, kind="ExternalInput").ap()
    moes = [nc.dram_tensor("moe%d" % e, [TPC, D], F32, kind="ExternalInput").ap()
            for e in range(8)]
    out_o = nc.dram_tensor("out", [TPC, D], F32, kind="ExternalOutput").ap()
    with contextlib.ExitStack() as st:
        a = sb(nc, st, "a", [128, D])
        bbuf = sb(nc, st, "b", [128, D])
        for ti in range(TPC // 128):
            r0 = ti * 128
            s.dma(a[:], x1[r0:r0 + 128, :])
            s.dma(bbuf[:], attn1[r0:r0 + 128, :])
            s.tt(a[:], a[:], bbuf[:], ALU.add)
            for e in range(8):
                s.dma(bbuf[:], moes[e][r0:r0 + 128, :])
                s.tt(a[:], a[:], bbuf[:], ALU.add)
            s.dma(out_o[r0:r0 + 128, :], a[:])
        s.finalize(nc)
    print("p6 instrs:", len(s.q), "sem total:", s.cnt)
    return nc


def run_p5_perm(inputs, xn1_full, probs_full):
    consts = make_consts()
    tvals = (np.arange(128, dtype=np.uint32)[:, None] * 256 +
             np.arange(256, dtype=np.uint32)[None, :])
    prefill = (NTOK + np.arange(MOE_CAP, dtype=np.uint32)).reshape(128, 40)
    nc = build_p5(0)
    in_maps = []
    for e in range(NCORES):
        pperm = np.ascontiguousarray(np.roll(probs_full, -e, axis=1))
        in_maps.append({
            "xn1f": xn1_full,
            "probsf": pperm,
            "w1": np.ascontiguousarray(inputs["L1_ffn_experts_inter"][e]),
            "w2": np.ascontiguousarray(inputs["L1_ffn_experts_out"][e]),
            "tvals": tvals,
            "tvals_f": tvals.astype(np.float32),
            "prefill": prefill,
            "c_su_mask": consts["su_mask"],
            "c_ident": consts["ident"],
        })
    res = run_bass_kernel_spmd(nc, in_maps, core_ids=list(range(NCORES)))
    return [r["moe"][:NTOK] for r in res.results]


def run_p6(xfull_x1, attn1_list, moe_list):
    nc = build_p6()
    in_maps = []
    for c in range(NCORES):
        m = {"x1": np.ascontiguousarray(xfull_x1[c * TPC:(c + 1) * TPC]),
             "attn1": attn1_list[c]}
        for e in range(8):
            m["moe%d" % e] = np.ascontiguousarray(moe_list[e][c * TPC:(c + 1) * TPC])
        in_maps.append(m)
    res = run_bass_kernel_spmd(nc, in_maps, core_ids=list(range(NCORES)))
    return [r["out"] for r in res.results]
